# version 02 (final)
# baseline (speedup 1.0000x reference)
import math
from contextlib import ExitStack

import numpy as np
import ml_dtypes

import concourse.bass as bass
import concourse.mybir as mybir
from concourse.bass_utils import run_bass_kernel_spmd

F32 = mybir.dt.float32
BF16 = mybir.dt.bfloat16
AF = mybir.ActivationFunctionType
ALU = mybir.AluOpType
AX = mybir.AxisListType

NCORE = 8
B, S, D = 2, 8192, 1024
TOK = (B * S) // NCORE
NT_OWN = TOK // 128
NT_BATCH = S // 128
NH, HD, NG = 16, 64, 8
CB = D // 128
E, F = 16, 2816
FC = F // 128
FF = 4096
LN_EPS = 1e-5
L1_EPS = 1e-12
QEPS = 1e-5

ENGS = ("sync", "scalar", "vector", "tensor", "gpsimd")

DEV = {
    "stop_after": None,
    "debug_outs": False,
    "nt_batch": None,
    "chain": True,
    "moe_experts": None,
    "moe_upto": None,
}


class Sem:
    def __init__(self, nc, es, name):
        self.h = es.enter_context(nc.semaphore(name))
        self.n = 0
        self.name = name

    def inc(self, k=1):
        self.n += k
        return self.n


class Prog:

    def __init__(self, nc, es):
        self.nc = nc
        self.es = es
        self.q = {e: [] for e in ENGS}
        self.last_wait = {e: {} for e in ENGS}
        self.sems = []
        self.self_sem = {"vector": self.sem("s_vv"), "scalar": self.sem("s_ss")}

    def sem(self, name):
        s = Sem(self.nc, self.es, name)
        self.sems.append(s)
        return s

    def chainwait(self, eng):
        self.wait(eng, self.self_sem[eng])

    def on(self, eng, fn):
        self.q[eng].append(fn)

    def wait(self, eng, sem, tgt=None):
        if tgt is None:
            tgt = sem.n
        if tgt <= 0:
            return
        lw = self.last_wait[eng].get(sem.name, -1)
        if tgt <= lw:
            return
        self.last_wait[eng][sem.name] = tgt
        self.q[eng].append(lambda e, h=sem.h, t=tgt: e.wait_ge(h, t))

    def do(self, eng, fn, inc=None, inc_k=1):
        chain = DEV["chain"] and eng in ("vector", "scalar") and inc is None
        if chain:
            inc = self.self_sem[eng]
            inc_k = 1
        if inc is not None:
            inc.inc(inc_k)

            def g(e, fn=fn, h=inc.h, k=inc_k):
                ins = fn(e)
                ins.then_inc(h, k)
            self.q[eng].append(g)
        else:
            self.q[eng].append(fn)
        if chain:
            self.wait(eng, inc)

    def dma(self, eng, out, in_, sem):
        self.do(eng, lambda e, o=out, i=in_: e.dma_start(out=o, in_=i), inc=sem, inc_k=16)

    def barrier(self, name_sem, engines=ENGS):
        base = name_sem.n
        for e in engines:
            name_sem.inc(1)
            self.q[e].append(lambda en, h=name_sem.h: en.nop().then_inc(h, 1))
        tgt = name_sem.n
        for e in engines:
            self.last_wait[e][name_sem.name] = tgt
            self.q[e].append(lambda en, h=name_sem.h, t=tgt: en.wait_ge(h, t))

    def emit(self, block):
        for e in ENGS:
            fns = self.q[e]
            sec = getattr(block, e)

            def run(engine, fns=fns):
                for f in fns:
                    f(engine)
            sec(run)


def bcast_row(p, row_ap, out_sbuf, psum, ones_c, s_mm, s_ve, width=None, out_dtype_copy=True):
    W = width if width is not None else out_sbuf.shape[-1]
    nblk = ceil_div(W, 512)
    for j in range(nblk):
        w = min(512, W - j * 512)
        pj = psum[j % len(psum)]
        p.do("tensor", lambda t, pj=pj, w=w, j=j, r=row_ap: t.matmul(
            out=pj[:, 0:w], lhsT=ones_c[:], rhs=r[0:1, j * 512:j * 512 + w],
            start=True, stop=True), inc=s_mm)
        p.wait("vector", s_mm)
        p.do("vector", lambda v, pj=pj, w=w, j=j: v.tensor_copy(
            out=out_sbuf[:, j * 512:j * 512 + w], in_=pj[:, 0:w]), inc=s_ve)


def ceil_div(a, b):
    return -(-a // b)



def build_launch_a():
    global NT_BATCH
    NT_BATCH = DEV["nt_batch"] or (S // 128)
    nc = bass.Bass("TRN2", target_bir_lowering=False)
    es = ExitStack()

    dram_in = lambda n, shp, dt=F32: nc.dram_tensor(n, shp, dt, kind="ExternalInput").ap()
    dram_out = lambda n, shp, dt=F32: nc.dram_tensor(n, shp, dt, kind="ExternalOutput").ap()
    dram_tmp = lambda n, shp, dt=F32: nc.dram_tensor(n, shp, dt, kind="Internal").ap()

    xb = dram_in("xb", [S, D])
    w_qkvT = dram_in("w_qkvT", [D, 3 * D])
    b_qkv = dram_in("b_qkv", [1, 3 * D])
    w_lepe = dram_in("w_lepe", [3, D])
    b_lepe = dram_in("b_lepe", [1, D])
    lepe_m = dram_in("lepe_m", [128, 2])
    router_wT = dram_in("router_wT", [D, E])
    gate_wT = dram_in("gate_wT", [E, D, F])
    up_wT = dram_in("up_wT", [E, D, F])
    down_wT = dram_in("down_wT", [E, F, D])
    wq1 = dram_in("wq1", [D, D])
    wk1 = dram_in("wk1", [D, D])
    wv1 = dram_in("wv1", [D, D])
    bqkv1 = dram_in("bqkv1", [1, 3 * D])
    rsin = dram_in("rsin", [S, D // 2])
    rcos = dram_in("rcos", [S, D // 2])
    ident16 = dram_in("ident16", [128, 128], BF16)
    ident32 = dram_in("ident32", [128, 128])
    ones_r = dram_in("ones_r", [1, 128])

    o_x1 = dram_out("o_x1", [TOK, D])
    o_qnT = dram_out("o_qnT", [D, TOK], BF16)
    o_x1nT = dram_out("o_x1nT", [D, TOK], BF16)
    o_kvp = dram_out("o_kvp", [HD, NH * HD])

    dbg = {}
    if DEV["debug_outs"]:
        dbg["xnT"] = dram_out("dbg_xnT", [CB, 128, TOK], BF16)
        dbg["xn0"] = dram_out("dbg_xn0", [128, D], BF16)
        dbg["st0"] = dram_out("dbg_st0", [128, 12])
        dbg["evt0"] = dram_out("dbg_evt0", [128, 1024], BF16)
        dbg["sqt0"] = dram_out("dbg_sqt0", [128, D])
        dbg["kT"] = dram_out("dbg_kT", [NG, 128, S], BF16)
        dbg["v65"] = dram_out("dbg_v65", [S, NH * 65], BF16)
        dbg["qT"] = dram_out("dbg_qT", [CB, 128, TOK], BF16)
        dbg["dw"] = dram_out("dbg_dw", [TOK, E])
        dbg["g"] = dram_out("dbg_g", [D, TOK])
        dbg["sctx"] = dram_out("dbg_sctx", [D, TOK])
        dbg["s"] = dram_out("dbg_s", [TOK, D])

    sc_xnT = dram_tmp("sc_xnT", [CB, 128, S], BF16)
    sc_kT = dram_tmp("sc_kT", [NG, 128, S], BF16)
    sc_v65 = dram_tmp("sc_v65", [S, NH * 65], BF16)
    sc_vwin = dram_tmp("sc_vwin", [(NT_OWN + 2) * 128, D])
    sc_sfeat = dram_tmp("sc_sfeat", [D, TOK])
    sc_qT = dram_tmp("sc_qT", [NG, 128, TOK], BF16)

    sb = lambda name, shape, dt: es.enter_context(nc.sbuf_tensor(name, shape, dt))
    xnT_own = sb("xnT_own", [128, CB * TOK], BF16)
    ones_c = sb("ones_c", [1, 128], F32)
    ones_c16 = sb("ones_c16", [1, 128], BF16)
    id16 = sb("id16", [128, 128], BF16)
    id32 = sb("id32", [128, 128], F32)
    es2 = ExitStack()
    sb2 = lambda name, shape, dt: es2.enter_context(nc.sbuf_tensor(name, shape, dt))

    xt = [sb2(f"xt{i}", [128, D], F32) for i in range(2)]
    sqt = [sb2(f"sqt{i}", [128, D], F32) for i in range(2)]
    xnt = [sb2(f"xnt{i}", [128, D], BF16) for i in range(2)]
    stat = [sb2(f"stat{i}", [128, 12], F32) for i in range(2)]
    evt = [sb2(f"evt{i}", [128, 1024], BF16) for i in range(2)]
    xnTt = [sb2(f"xnTt{i}", [128, CB * 128], BF16) for i in range(2)]
    kvf = [sb2(f"kvf{i}", [128, 2 * D], F32) for i in range(2)]
    qf = [sb2(f"qf{i}", [128, D], F32) for i in range(2)]
    tmpa = sb2("tmpa", [128, D], F32)
    tmpb = sb2("tmpb", [128, D], F32)
    tmpc = sb2("tmpc", [128, D], F32)
    ropetS = [sb2(f"ropetS{i}", [128, D // 2], F32) for i in range(2)]
    ropetC = [sb2(f"ropetC{i}", [128, D // 2], F32) for i in range(2)]
    k16 = sb2("k16", [128, D], BF16)
    q16t = sb2("q16t", [128, D], BF16)
    v16 = sb2("v16", [128, NH * 65], BF16)
    bqkv_b = sb2("bqkv_b", [128, 3 * D], F32)

    ps = [es.enter_context(nc.psum_tensor(f"ps{i}", [128, 512], F32)) for i in range(6)]
    pt16 = [es.enter_context(nc.psum_tensor(f"pt16_{i}", [128, 1024], BF16)) for i in range(2)]

    p = Prog(nc, es)
    s_ld = p.sem("s_ld")
    s_wl = p.sem("s_wl")
    s_sc = p.sem("s_sc")
    s_ve = p.sem("s_ve")
    s_mm = p.sem("s_mm")
    s_tr = p.sem("s_tr")
    s_dw = p.sem("s_dw")
    s_st = p.sem("s_st")
    s_p0 = p.sem("s_p0")
    s_jn = p.sem("s_jn")

    block_holder = {}

    p.dma("sync", ones_c[:], ones_r, s_ld)
    p.dma("sync", id16[:], ident16, s_ld)
    p.dma("sync", id32[:], ident32, s_ld)
    p.dma("sync", bqkv_b[0:1, :], b_qkv, s_ld)
    p.wait("vector", s_ld)
    p.do("vector", lambda v: v.tensor_copy(out=ones_c16[:], in_=ones_c[:]), inc=s_ve)
    p.wait("tensor", s_ld)
    for j in range(6):
        p.do("tensor", lambda t, j=j: t.matmul(
            out=ps[j][:], lhsT=ones_c[:], rhs=bqkv_b[0:1, j * 512:(j + 1) * 512],
            start=True, stop=True), inc=s_mm)
    p.wait("vector", s_mm)
    for j in range(6):
        p.do("vector", lambda v, j=j: v.tensor_copy(
            out=bqkv_b[:, j * 512:(j + 1) * 512], in_=ps[j][:]), inc=s_ve)
    p.barrier(s_jn)

    ld_base = {}
    ve_xn = {}
    ve_done = {}
    tr_done = {}
    dw_p0 = {}
    for i in range(NT_BATCH):
        bi = i % 2
        if i >= 2:
            p.wait("sync", s_p0, ve_xn[i - 2])
        p.dma("sync", xt[bi][:], xb[i * 128:(i + 1) * 128, :], s_ld)
        ld_base[i] = s_ld.n
        p.wait("scalar", s_ld, ld_base[i])
        if i >= 2:
            p.wait("scalar", s_p0, ve_xn[i - 2])
        p.do("scalar", lambda s, bi=bi: s.activation(out=sqt[bi][:], in_=xt[bi][:], func=AF.Square), inc=s_sc)
        sc_sq = s_sc.n
        p.wait("vector", s_ld, ld_base[i])
        st = stat[bi]
        p.do("vector", lambda v, bi=bi, st=st: v.reduce_sum(out=st[:, 0:1], in_=xt[bi][:], axis=AX.X), inc=s_ve)
        p.wait("vector", s_sc, sc_sq)
        p.do("vector", lambda v, bi=bi, st=st: v.reduce_sum(out=st[:, 1:2], in_=sqt[bi][:], axis=AX.X), inc=s_ve)
        p.do("vector", lambda v, st=st: v.tensor_scalar(out=st[:, 2:3], in0=st[:, 0:1], scalar1=1.0 / D, scalar2=None, op0=ALU.mult))
        p.do("vector", lambda v, st=st: v.tensor_scalar(out=st[:, 3:4], in0=st[:, 1:2], scalar1=1.0 / D, scalar2=None, op0=ALU.mult))
        p.do("vector", lambda v, st=st: v.tensor_tensor(out=st[:, 8:9], in0=st[:, 2:3], in1=st[:, 2:3], op=ALU.mult))
        p.do("vector", lambda v, st=st: v.tensor_tensor(out=st[:, 9:10], in0=st[:, 3:4], in1=st[:, 8:9], op=ALU.subtract))
        p.do("vector", lambda v, st=st: v.tensor_scalar(out=st[:, 4:5], in0=st[:, 9:10], scalar1=LN_EPS, scalar2=None, op0=ALU.add), inc=s_ve)
        ve_var = s_ve.n
        p.wait("scalar", s_ve, ve_var)
        p.do("scalar", lambda s, st=st: s.activation(out=st[:, 5:6], in_=st[:, 4:5], func=AF.Sqrt), inc=s_sc)
        sc_sqrt = s_sc.n
        p.wait("vector", s_sc, sc_sqrt)
        p.do("vector", lambda v, st=st: v.reciprocal(out=st[:, 6:7], in_=st[:, 5:6]))
        p.do("vector", lambda v, st=st: v.tensor_scalar(out=st[:, 7:8], in0=st[:, 6:7], scalar1=st[:, 2:3], scalar2=-1.0, op0=ALU.mult, op1=ALU.mult))
        p.do("vector", lambda v, bi=bi, st=st: v.tensor_scalar(out=xnt[bi][:], in0=xt[bi][:], scalar1=st[:, 6:7], scalar2=st[:, 7:8], op0=ALU.mult, op1=ALU.add), inc=s_p0)
        ve_xn[i] = s_p0.n
        p.wait("tensor", s_p0, ve_xn[i])
        if i >= 2:
            p.wait("tensor", s_ve, ve_done[i - 2])
        for j in range(CB):
            p.do("tensor", lambda t, bi=bi, j=j: t.transpose(
                out=pt16[bi][:, j * 128:(j + 1) * 128], in_=xnt[bi][:, j * 128:(j + 1) * 128],
                identity=id16[:]), inc=(s_tr if j == CB - 1 else None))
        tr_done[i] = s_tr.n
        p.wait("vector", s_tr, tr_done[i])
        if i >= 2:
            p.wait("vector", s_dw, dw_p0[i - 2])
        p.do("vector", lambda v, bi=bi: v.tensor_copy(out=evt[bi][:, 0:512], in_=pt16[bi][:, 0:512]))
        p.do("vector", lambda v, bi=bi: v.tensor_copy(out=evt[bi][:, 512:1024], in_=pt16[bi][:, 512:1024]), inc=s_ve)
        ve_done[i] = s_ve.n
        p.wait("gpsimd", s_ve, ve_done[i])
        p.dma("gpsimd", sc_xnT.rearrange("cb p s -> p cb s")[:, :, i * 128:(i + 1) * 128],
              evt[bi][:].rearrange("p (cb t) -> p cb t", cb=CB), s_dw)
        dw_p0[i] = s_dw.n
        if i == 0 and DEV["debug_outs"]:
            p.dma("gpsimd", dbg["sqt0"], sqt[0][:], s_dw)
            p.dma("gpsimd", dbg["xn0"], xnt[0][:], s_dw)
            p.dma("gpsimd", dbg["st0"], stat[0][:], s_dw)
            p.dma("gpsimd", dbg["evt0"], evt[0][:], s_dw)
        if i < NT_OWN:
            p.do("vector", lambda v, bi=bi, i=i: v.tensor_copy(
                out=xnT_own[:].rearrange("p (cb t) -> p cb t", cb=CB)[:, :, i * 128:(i + 1) * 128],
                in_=evt[bi][:].rearrange("p (cb t) -> p cb t", cb=CB)), inc=s_ve)
            ve_done[i] = s_ve.n

    p.barrier(s_jn)
    if DEV["debug_outs"]:
        p.dma("gpsimd", dbg["xnT"], sc_xnT[:, :, 0:TOK], s_dw)
        p.wait("gpsimd", s_dw)
    if DEV["stop_after"] == "P0":
        p.barrier(s_jn)
        block = es.enter_context(nc.Block())
        p.emit(block)
        es2.close()
        es.close()
        return nc

    wqkv16 = sb2("wqkv16", [128, CB * 3 * D], BF16)
    p.dma("gpsimd", wqkv16[:].rearrange("p (cb o) -> p cb o", cb=CB),
          w_qkvT.rearrange("(cb p) o -> p cb o", p=128), s_wl)
    wq_ld = s_wl.n
    p.wait("tensor", s_wl, wq_ld)
    p.do("vector", lambda v: v.memset(v16[:], 1.0), inc=s_ve)

    ve_p2 = {}
    sc_p2 = {}
    ld_p2 = {}
    dw_p2 = {}
    for i in range(NT_BATCH):
        bi = i % 2
        own = i < NT_OWN
        if i >= 2:
            p.wait("gpsimd", s_mm, ve_p2.get(("mmdone", i - 2), 0))
        p.dma("gpsimd", xnTt[bi][:].rearrange("p (cb t) -> p cb t", cb=CB),
              sc_xnT[:, :, i * 128:(i + 1) * 128].rearrange("cb p t -> p cb t"), s_wl)
        ld_p2[i] = s_wl.n
        if i >= 2:
            p.wait("sync", s_ve, ve_p2.get(("rope_used", i - 2), 0))
        p.dma("sync", ropetS[bi][:], rsin[i * 128:(i + 1) * 128, :], s_ld)
        p.dma("sync", ropetC[bi][:], rcos[i * 128:(i + 1) * 128, :], s_ld)
        ld_rope = s_ld.n
        p.wait("tensor", s_wl, ld_p2[i])
        if i >= 1:
            p.wait("tensor", s_ve, ve_p2.get(("evac", i - 1), 0))
        nblk = 6 if own else 4
        for blk in range(nblk):
            ocol = (D + blk * 512) if blk < 4 else ((blk - 4) * 512)
            for cb in range(CB):
                p.do("tensor", lambda t, bi=bi, cb=cb, blk=blk, ocol=ocol: t.matmul(
                    out=ps[blk][:],
                    lhsT=xnTt[bi][:, cb * 128:(cb + 1) * 128],
                    rhs=wqkv16[:, cb * 3 * D + ocol: cb * 3 * D + ocol + 512],
                    start=(cb == 0), stop=(cb == CB - 1)),
                    inc=(s_mm if cb == CB - 1 else None))
        ve_p2[("mmdone", i)] = s_mm.n
        mm_done = s_mm.n
        p.wait("vector", s_mm, mm_done)
        for blk in range(4):
            p.do("vector", lambda v, bi=bi, blk=blk: v.tensor_tensor(
                out=kvf[bi][:, blk * 512:(blk + 1) * 512], in0=ps[blk][:],
                in1=bqkv_b[:, D + blk * 512: D + (blk + 1) * 512], op=ALU.add))
        if own:
            for blk in range(2):
                p.do("vector", lambda v, bi=bi, blk=blk: v.tensor_tensor(
                    out=qf[bi][:, blk * 512:(blk + 1) * 512], in0=ps[4 + blk][:],
                    in1=bqkv_b[:, blk * 512:(blk + 1) * 512], op=ALU.add))
        p.do("vector", lambda v: v.nop(), inc=s_ve)
        ve_p2[("evac", i)] = s_ve.n
        p.do("vector", lambda v, bi=bi: v.tensor_scalar(out=tmpa[:], in0=kvf[bi][:, 0:D], scalar1=0.0, scalar2=None, op0=ALU.min), inc=s_ve)
        ve_min = s_ve.n
        p.wait("scalar", s_ve, ve_min)
        p.do("scalar", lambda s: s.activation(out=tmpb[:], in_=tmpa[:], func=AF.Exp), inc=s_sc)
        sc_exp = s_sc.n
        p.do("vector", lambda v, bi=bi: v.tensor_scalar(out=tmpc[:], in0=kvf[bi][:, 0:D], scalar1=0.0, scalar2=None, op0=ALU.max))
        p.wait("vector", s_sc, sc_exp)
        p.do("vector", lambda v, bi=bi: v.tensor_tensor(out=kvf[bi][:, 0:D], in0=tmpb[:], in1=tmpc[:], op=ALU.add))
        p.wait("vector", s_ld, ld_rope)

        def plan_rope(src, dst, bi, scale):
            HW = D // 2
            se = src[:, 0:D:2]
            so = src[:, 1:D:2]
            p.do("vector", lambda v, se=se, bi=bi: v.tensor_tensor(out=tmpa[:, 0:HW], in0=se, in1=ropetC[bi][:], op=ALU.mult))
            p.do("vector", lambda v, so=so, bi=bi: v.tensor_tensor(out=tmpa[:, HW:D], in0=so, in1=ropetS[bi][:], op=ALU.mult))
            p.do("vector", lambda v: v.tensor_tensor(out=tmpb[:, 0:HW], in0=tmpa[:, 0:HW], in1=tmpa[:, HW:D], op=ALU.subtract))
            p.do("vector", lambda v, se=se, bi=bi: v.tensor_tensor(out=tmpa[:, 0:HW], in0=se, in1=ropetS[bi][:], op=ALU.mult))
            p.do("vector", lambda v, so=so, bi=bi: v.tensor_tensor(out=tmpa[:, HW:D], in0=so, in1=ropetC[bi][:], op=ALU.mult))
            p.do("vector", lambda v: v.tensor_tensor(out=tmpb[:, HW:D], in0=tmpa[:, 0:HW], in1=tmpa[:, HW:D], op=ALU.add))
            p.do("vector", lambda v, dst=dst, scale=scale: v.tensor_scalar(out=dst[:, 0:D:2], in0=tmpb[:, 0:HW], scalar1=scale, scalar2=None, op0=ALU.mult))
            p.do("vector", lambda v, dst=dst, scale=scale: v.tensor_scalar(out=dst[:, 1:D:2], in0=tmpb[:, HW:D], scalar1=scale, scalar2=None, op0=ALU.mult), inc=s_ve)

        plan_rope(kvf[bi][:, 0:D], k16, bi, 1.0)
        ve_k16 = s_ve.n
        p.do("vector", lambda v, bi=bi: v.tensor_copy(
            out=v16[:].rearrange("p (h j) -> p h j", h=NH)[:, :, 0:64],
            in_=kvf[bi][:, D:2 * D].rearrange("p (h j) -> p h j", h=NH)), inc=s_ve)
        ve_v16 = s_ve.n
        if own:
            p.do("vector", lambda v, bi=bi: v.tensor_scalar(out=tmpa[:], in0=qf[bi][:], scalar1=0.0, scalar2=None, op0=ALU.min), inc=s_ve)
            p.wait("scalar", s_ve)
            p.do("scalar", lambda s: s.activation(out=tmpb[:], in_=tmpa[:], func=AF.Exp), inc=s_sc)
            p.do("vector", lambda v, bi=bi: v.tensor_scalar(out=tmpc[:], in0=qf[bi][:], scalar1=0.0, scalar2=None, op0=ALU.max))
            p.wait("vector", s_sc)
            p.do("vector", lambda v, bi=bi: v.tensor_tensor(out=qf[bi][:], in0=tmpb[:], in1=tmpc[:], op=ALU.add))
            plan_rope(qf[bi][:], q16t, bi, 0.125)
        p.do("vector", lambda v: v.nop(), inc=s_ve)
        ve_p2[("rope_used", i)] = s_ve.n
        ve_ops_done = s_ve.n
        p.wait("tensor", s_ve, ve_ops_done)
        if i >= 1:
            p.wait("tensor", s_ve, ve_p2.get(("tevac", i - 1), 0))
        for j in range(CB):
            p.do("tensor", lambda t, j=j: t.transpose(
                out=pt16[0][:, j * 128:(j + 1) * 128], in_=k16[:, j * 128:(j + 1) * 128],
                identity=id16[:]), inc=(s_tr if j == CB - 1 else None))
        if own:
            for j in range(CB):
                p.do("tensor", lambda t, j=j: t.transpose(
                    out=pt16[1][:, j * 128:(j + 1) * 128], in_=q16t[:, j * 128:(j + 1) * 128],
                    identity=id16[:]), inc=(s_tr if j == CB - 1 else None))
        tr_p2 = s_tr.n
        p.wait("vector", s_tr, tr_p2)
        p.do("vector", lambda v: v.tensor_copy(out=evt[0][:, 0:512], in_=pt16[0][:, 0:512]))
        p.do("vector", lambda v: v.tensor_copy(out=evt[0][:, 512:1024], in_=pt16[0][:, 512:1024]), inc=s_ve)
        if own:
            p.do("vector", lambda v: v.tensor_copy(out=evt[1][:, 0:512], in_=pt16[1][:, 0:512]))
            p.do("vector", lambda v: v.tensor_copy(out=evt[1][:, 512:1024], in_=pt16[1][:, 512:1024]), inc=s_ve)
        ve_p2[("tevac", i)] = s_ve.n
        ve_tevac = s_ve.n
        p.wait("gpsimd", s_ve, ve_tevac)
        p.dma("gpsimd", sc_kT.rearrange("g p s -> p g s")[:, :, i * 128:(i + 1) * 128],
              evt[0][:].rearrange("p (g t) -> p g t", g=NG), s_dw)
        if own:
            p.dma("gpsimd", sc_qT.rearrange("g p s -> p g s")[:, :, i * 128:(i + 1) * 128],
                  evt[1][:].rearrange("p (g t) -> p g t", g=NG), s_dw)
        p.dma("gpsimd", sc_v65[i * 128:(i + 1) * 128, :], v16[:], s_dw)
        if i <= NT_OWN or i == NT_BATCH - 1:
            slot = 0 if i == NT_BATCH - 1 else i + 1
            p.wait("gpsimd", s_ve, ve_p2[("evac", i)])
            p.dma("gpsimd", sc_vwin[slot * 128:(slot + 1) * 128, :], kvf[bi][:, D:2 * D], s_dw)
        dw_p2[i] = s_dw.n
        p.wait("vector", s_dw, dw_p2[i])

    p.barrier(s_jn)

    if DEV["debug_outs"]:
        p.dma("gpsimd", dbg["kT"], sc_kT, s_dw)
        p.dma("gpsimd", dbg["v65"], sc_v65, s_dw)
        p.dma("gpsimd", dbg["qT"], sc_qT, s_dw)
        p.wait("gpsimd", s_dw)
    if DEV["stop_after"] == "P2":
        p.barrier(s_jn)
        block = es.enter_context(nc.Block())
        p.emit(block)
        es2.close()
        es.close()
        return nc

    es2.close()
    es3 = ExitStack()
    sb3 = lambda name, shape, dt: es3.enter_context(nc.sbuf_tensor(name, shape, dt))
    sc_acc = dram_tmp("sc_acc", [D, TOK])
    rw16 = sb3("rw16", [128, CB * E], BF16)
    lgbuf = sb3("lgbuf", [128, NT_OWN * 16], F32)
    lgm = sb3("lgm", [128, 16], F32)
    rcols = sb3("rcols", [128, 8], F32)
    dwbuf = sb3("dwbuf", [128, NT_OWN * 16], F32)
    dwT = sb3("dwT", [16, TOK], F32)
    st_dw = sb3("st_dw", [1, TOK], F32)
    dwb = sb3("dwb", [128, TOK], BF16)
    gwt = [sb3(f"gwt{i}", [128, CB * 128], BF16) for i in range(2)]
    uwt = [sb3(f"uwt{i}", [128, CB * 128], BF16) for i in range(2)]
    dnt = [sb3(f"dnt{i}", [128, D], BF16) for i in range(2)]
    h_all = sb3("h_all", [128, FC * TOK], BF16)
    sgbuf = sb3("sgbuf", [128, 512], BF16)
    hstg = sb3("hstg", [128, 512], F32)

    p.dma("gpsimd", rw16[:].rearrange("p (cb e) -> p cb e", cb=CB),
          router_wT.rearrange("(cb p) e -> p cb e", p=128), s_wl)
    p.wait("tensor", s_wl)

    for i in range(NT_OWN):
        bank = i % 2
        if i >= 2:
            p.wait("tensor", s_ve, ve_p2.get(("lg", i - 2), 0))
        for cb in range(CB):
            p.do("tensor", lambda t, i=i, cb=cb, bank=bank: t.matmul(
                out=ps[bank][:, 0:16],
                lhsT=xnT_own[:, cb * TOK + i * 128: cb * TOK + (i + 1) * 128],
                rhs=rw16[:, cb * E:(cb + 1) * E],
                start=(cb == 0), stop=(cb == CB - 1)),
                inc=(s_mm if cb == CB - 1 else None))
        p.wait("vector", s_mm)
        p.do("vector", lambda v, i=i, bank=bank: v.tensor_copy(
            out=lgbuf[:, i * 16:(i + 1) * 16], in_=ps[bank][:, 0:16]), inc=s_ve)
        ve_p2[("lg", i)] = s_ve.n
    lgt = sb3("lgt", [128, 16], F32)
    for i in range(NT_OWN):
        L = lgbuf[:, i * 16:(i + 1) * 16]
        p.do("vector", lambda v, L=L: v.tensor_copy(out=lgm[:], in_=L))
        for j in range(8):
            dstc = 0 if j == 0 else (1 if j == 7 else 5)
            p.do("vector", lambda v, dstc=dstc: v.reduce_max(
                out=rcols[:, dstc:dstc + 1], in_=lgm[:], axis=AX.X))
            if j < 7:
                p.do("vector", lambda v, dstc=dstc: v.tensor_scalar(
                    out=lgt[:], in0=lgm[:], scalar1=rcols[:, dstc:dstc + 1], scalar2=None, op0=ALU.is_ge))
                p.do("vector", lambda v: v.scalar_tensor_tensor(
                    out=lgm[:], in0=lgt[:], scalar=-2e30, in1=lgm[:], op0=ALU.mult, op1=ALU.add))
        p.do("vector", lambda v, L=L: v.tensor_scalar(
            out=lgt[:], in0=L, scalar1=rcols[:, 1:2], scalar2=None, op0=ALU.is_ge))
        p.do("vector", lambda v: v.tensor_scalar(
            out=rcols[:, 2:3], in0=rcols[:, 0:1], scalar1=-1.0, scalar2=None, op0=ALU.mult), inc=s_ve)
        p.wait("scalar", s_ve)
        p.do("scalar", lambda s, L=L, i=i: s.activation(
            out=dwbuf[:, i * 16:(i + 1) * 16], in_=L, func=AF.Exp, bias=rcols[:, 2:3]), inc=s_sc)
        p.wait("vector", s_sc)
        W = dwbuf[:, i * 16:(i + 1) * 16]
        p.do("vector", lambda v, W=W: v.tensor_tensor(out=W, in0=W, in1=lgt[:], op=ALU.mult))
        p.do("vector", lambda v, W=W: v.reduce_sum(out=rcols[:, 3:4], in_=W, axis=AX.X))
        p.do("vector", lambda v: v.reciprocal(out=rcols[:, 4:5], in_=rcols[:, 3:4]))
        p.do("vector", lambda v, W=W: v.tensor_scalar(
            out=W, in0=W, scalar1=rcols[:, 4:5], scalar2=None, op0=ALU.mult), inc=s_ve)
    ve_dw = s_ve.n
    p.wait("tensor", s_ve, ve_dw)
    for i in range(NT_OWN):
        grp, sl = divmod(i, 4)
        if sl == 0 and grp >= 2:
            p.wait("tensor", s_ve, ve_p2.get(("dwT", grp - 2), 0))
        p.do("tensor", lambda t, i=i, grp=grp, sl=sl: t.transpose(
            out=ps[grp % 2][0:16, sl * 128:(sl + 1) * 128],
            in_=dwbuf[:, i * 16:(i + 1) * 16], identity=id32[:]),
            inc=(s_tr if sl == 3 else None))
        if sl == 3:
            tr_n = s_tr.n
            p.wait("vector", s_tr, tr_n)
            p.do("vector", lambda v, grp=grp: v.tensor_copy(
                out=dwT[:, grp * 512:(grp + 1) * 512], in_=ps[grp % 2][0:16, :]), inc=s_ve)
            ve_p2[("dwT", grp)] = s_ve.n
    p.barrier(s_jn)
    if DEV["debug_outs"]:
        p.dma("gpsimd", dbg["dw"].rearrange("(i t) e -> t i e", t=128), dwbuf[:].rearrange("t (i e) -> t i e", e=E), s_dw)
        p.wait("gpsimd", s_dw)
    if DEV["stop_after"] == "P3r":
        p.barrier(s_jn)
        block = es.enter_context(nc.Block())
        p.emit(block)
        es3.close()
        es.close()
        return nc

    dn_all = sb3("dn_all", [128, FC * D], BF16)
    ve_h = {}
    dn_done = {}
    for e in range(DEV.get("moe_experts") or E):
        p.wait("gpsimd", s_ve, ve_dw)
        if e >= 1:
            p.wait("gpsimd", s_ve, ve_p2[("dwb", e - 1)])
        p.dma("gpsimd", st_dw[:], dwT[e:e + 1, :], s_wl)
        row_ld = s_wl.n
        p.wait("tensor", s_wl, row_ld)
        if e >= 1:
            p.wait("tensor", s_ve, ve_p2[("dne", e - 1, 31)])
            p.wait("tensor", s_mm, dn_done[e - 1])
        dwb_cp = {}
        for j in range(4):
            bk = 4 + (j % 2)
            if j >= 2:
                p.wait("tensor", s_ve, dwb_cp[j - 2])
            p.do("tensor", lambda t, j=j, bk=bk: t.matmul(
                out=ps[bk][:], lhsT=ones_c[:], rhs=st_dw[0:1, j * 512:(j + 1) * 512],
                start=True, stop=True), inc=s_mm)
            p.wait("vector", s_mm)
            p.do("vector", lambda v, j=j, bk=bk: v.tensor_copy(
                out=dwb[:, j * 512:(j + 1) * 512], in_=ps[bk][:]), inc=s_ve)
            dwb_cp[j] = s_ve.n
        ve_p2[("dwb", e)] = s_ve.n
        if DEV["moe_upto"] == "dwb":
            continue
        if e >= 1:
            p.wait("vector", s_mm, dn_done[e - 1])
        for fc in range(FC):
            wb = fc % 2
            if (e, fc) > (0, 1):
                guard = ve_p2.get(("guw", e, fc - 2), ve_p2.get(("guw", e - 1, FC + fc - 2), 0))
                p.wait("gpsimd", s_mm, guard)
            p.dma("gpsimd", gwt[wb][:].rearrange("p (cb f) -> p cb f", cb=CB),
                  gate_wT[e, :, fc * 128:(fc + 1) * 128].rearrange("(cb p) f -> p cb f", p=128), s_wl)
            p.dma("gpsimd", uwt[wb][:].rearrange("p (cb f) -> p cb f", cb=CB),
                  up_wT[e, :, fc * 128:(fc + 1) * 128].rearrange("(cb p) f -> p cb f", p=128), s_wl)
            wl_n = s_wl.n
            p.wait("tensor", s_wl, wl_n)
            for tb in range(4):
                pg, pu = ps[0 + 2 * (tb % 2)], ps[1 + 2 * (tb % 2)]
                guard = ve_h.get((e, fc, tb - 2),
                                 ve_h.get((e, fc - 1, tb + 2),
                                          ve_h.get((e - 1, FC - 1, tb + 2), 0)))
                p.wait("tensor", s_ve, guard)
                for cb in range(CB):
                    p.do("tensor", lambda t, cb=cb, tb=tb, wb=wb, pg=pg: t.matmul(
                        out=pg[:], lhsT=gwt[wb][:, cb * 128:(cb + 1) * 128],
                        rhs=xnT_own[:, cb * TOK + tb * 512: cb * TOK + tb * 512 + 512],
                        start=(cb == 0), stop=(cb == CB - 1)))
                for cb in range(CB):
                    p.do("tensor", lambda t, cb=cb, tb=tb, wb=wb, pu=pu: t.matmul(
                        out=pu[:], lhsT=uwt[wb][:, cb * 128:(cb + 1) * 128],
                        rhs=xnT_own[:, cb * TOK + tb * 512: cb * TOK + tb * 512 + 512],
                        start=(cb == 0), stop=(cb == CB - 1)),
                        inc=(s_mm if cb == CB - 1 else None))
                mm_gu = s_mm.n
                if tb == 3:
                    ve_p2[("guw", e, fc)] = mm_gu
                p.wait("scalar", s_mm, mm_gu)
                p.wait("scalar", s_ve, ve_h.get((e, fc, tb - 1),
                                                ve_h.get((e, fc - 1, 3),
                                                         ve_h.get((e - 1, FC - 1, 3), 0))))
                p.do("scalar", lambda s, pg=pg: s.activation(
                    out=sgbuf[:], in_=pg[:], func=AF.Silu), inc=s_sc)
                p.wait("vector", s_sc)
                hd = h_all[:, fc * TOK + tb * 512: fc * TOK + tb * 512 + 512]
                p.do("vector", lambda v, pu=pu: v.tensor_tensor(
                    out=hstg[:], in0=sgbuf[:], in1=pu[:], op=ALU.mult))
                p.do("vector", lambda v, hd=hd, tb=tb: v.tensor_tensor(
                    out=hd, in0=hstg[:], in1=dwb[:, tb * 512:(tb + 1) * 512], op=ALU.mult), inc=s_ve)
                ve_h[(e, fc, tb)] = s_ve.n
        h_done = s_ve.n
        if DEV["moe_upto"] == "h":
            dn_done[e] = s_mm.n
            continue
        if e >= 1:
            p.wait("gpsimd", s_mm, dn_done[e - 1])
        for half in range(2):
            p.dma("gpsimd",
                  dn_all[:].rearrange("p (fc o) -> p fc o", fc=FC)[:, half * 11:(half + 1) * 11, :],
                  down_wT[e, half * 11 * 128:(half + 1) * 11 * 128, :].rearrange("(fc p) o -> p fc o", p=128),
                  s_wl)
        dnl = s_wl.n
        p.wait("tensor", s_ve, h_done)
        p.wait("tensor", s_wl, dnl)
        for cc in range(CB):
            for tb in range(4):
                u = cc * 4 + tb
                bank = u % 2
                if u >= 2:
                    p.wait("tensor", s_ve, ve_p2.get(("dne", e, u - 2), 0))
                for fc in range(FC):
                    p.do("tensor", lambda t, fc=fc, cc=cc, tb=tb, bank=bank: t.matmul(
                        out=ps[4 + bank][:], lhsT=dn_all[:, fc * D + cc * 128: fc * D + (cc + 1) * 128],
                        rhs=h_all[:, fc * TOK + tb * 512: fc * TOK + tb * 512 + 512],
                        start=(fc == 0), stop=(fc == FC - 1)),
                        inc=(s_mm if fc == FC - 1 else None))
                mm_dn = s_mm.n
                p.wait("vector", s_mm, mm_dn)
                p.do("vector", lambda v, bank=bank: v.tensor_copy(out=hstg[:], in_=ps[4 + bank][:]), inc=s_ve)
                ve_p2[("dne", e, u)] = s_ve.n
                p.wait("gpsimd", s_ve, s_ve.n)
                dst = sc_acc[cc * 128:(cc + 1) * 128, tb * 512:(tb + 1) * 512]
                if e == 0:
                    p.dma("gpsimd", dst, hstg[:], s_dw)
                else:
                    p.do("gpsimd", lambda g, dst=dst: g.dma_start(
                        out=dst, in_=hstg[:], accum_op=ALU.add), inc=s_dw, inc_k=16)
                p.wait("vector", s_dw, s_dw.n)
        dn_done[e] = s_mm.n

    p.barrier(s_jn)
    if DEV["debug_outs"]:
        p.dma("gpsimd", dbg["g"], sc_acc, s_dw)
        p.wait("gpsimd", s_dw)
    if DEV["stop_after"] == "P3":
        p.barrier(s_jn)
        block = es.enter_context(nc.Block())
        p.emit(block)
        es3.close()
        es.close()
        return nc

    es3.close()
    es4 = ExitStack()
    sb4 = lambda name, shape, dt: es4.enter_context(nc.sbuf_tensor(name, shape, dt))
    kTg = [sb4(f"kTg{i}", [128, S], BF16) for i in range(2)]
    v65g = [sb4(f"v65g{i}", [128, (S // 128) * 130], BF16) for i in range(2)]
    qTg = [sb4(f"qTg{i}", [128, TOK], BF16) for i in range(2)]
    ptile = [sb4(f"ptile{i}", [128, 512], BF16) for i in range(2)]
    lrbuf = sb4("lrbuf", [1, 512], F32)
    cbuf = sb4("cbuf", [64, 512], F32)
    sfp = sb4("sfp", [64, 512], F32)

    ld_g = {}
    fin_g = {}
    for g in range(NG):
        gb = g % 2
        if g >= 2:
            p.wait("gpsimd", s_mm, fin_g[g - 2])
        p.dma("gpsimd", kTg[gb][:], sc_kT[g], s_wl)
        p.dma("gpsimd", qTg[gb][:], sc_qT[g], s_wl)
        for half in range(8):
            rows = S // (8 * 128)
            p.dma("gpsimd",
                  v65g[gb][:].rearrange("p (kc c) -> p kc c", c=130)[:, half * 8:(half + 1) * 8, :],
                  sc_v65[half * 1024:(half + 1) * 1024, 2 * g * 65:(2 * g + 2) * 65]
                  .rearrange("(kc p) c -> p kc c", p=128),
                  s_wl)
        g_ld = s_wl.n
        p.wait("tensor", s_wl, g_ld)
        p.wait("scalar", s_wl, g_ld)
        exp_done = {}
        unit = 0
        for q5 in range(4):
            for kb in range(16):
                for h2 in range(2):
                    for kc in range(4):
                        qkb = unit % 2
                        if unit >= 2:
                            p.wait("tensor", s_sc, exp_done[unit - 2])
                        p.do("tensor", lambda t, gb=gb, h2=h2, kb=kb, kc=kc, q5=q5, qkb=qkb: t.matmul(
                            out=ps[qkb][:],
                            lhsT=kTg[gb][h2 * 64:(h2 + 1) * 64, kb * 512 + kc * 128: kb * 512 + (kc + 1) * 128],
                            rhs=qTg[gb][h2 * 64:(h2 + 1) * 64, q5 * 512:(q5 + 1) * 512],
                            start=True, stop=True), inc=s_mm)
                        mm_qk = s_mm.n
                        p.wait("scalar", s_mm, mm_qk)
                        if unit >= 2:
                            p.wait("scalar", s_mm, exp_done.get(("pv", unit - 2), 0))
                        p.do("scalar", lambda s, qkb=qkb, pb=unit % 2: s.activation(
                            out=ptile[pb][:], in_=ps[qkb][:], func=AF.Exp), inc=s_sc)
                        exp_done[unit] = s_sc.n
                        p.wait("tensor", s_sc, exp_done[unit])
                        p.do("tensor", lambda t, gb=gb, h2=h2, kb=kb, kc=kc, pb=unit % 2: t.matmul(
                            out=ps[2 + h2][0:65, :],
                            lhsT=v65g[gb][:, (kb * 4 + kc) * 130 + h2 * 65: (kb * 4 + kc) * 130 + h2 * 65 + 65],
                            rhs=ptile[pb][:],
                            start=(kb == 0 and kc == 0), stop=(kb == 15 and kc == 3),
                            skip_group_check=True), inc=s_mm)
                        exp_done[("pv", unit)] = s_mm.n
                        unit += 1
            for h2 in range(2):
                p.wait("vector", s_mm)
                p.do("vector", lambda v, h2=h2: v.reciprocal(out=lrbuf[:], in_=ps[2 + h2][64:65, :]), inc=s_ve)
                lr_n = s_ve.n
                p.wait("tensor", s_ve, lr_n)
                p.do("tensor", lambda t: t.matmul(
                    out=ps[4][:], lhsT=ones_c[:], rhs=lrbuf[:], start=True, stop=True), inc=s_mm)
                bc_n = s_mm.n
                p.do("vector", lambda v, h2=h2: v.tensor_copy(out=cbuf[:], in_=ps[2 + h2][0:64, :]), inc=s_ve)
                p.wait("vector", s_mm, bc_n)
                p.do("vector", lambda v: v.tensor_tensor(out=sfp[:], in0=cbuf[:], in1=ps[4][0:64, :], op=ALU.mult), inc=s_ve)
                sf_n = s_ve.n
                p.wait("gpsimd", s_ve, sf_n)
                p.dma("gpsimd", sc_sfeat[(2 * g + h2) * 64:(2 * g + h2 + 1) * 64, q5 * 512:(q5 + 1) * 512], sfp[:], s_dw)
                p.wait("vector", s_dw, s_dw.n)
                p.wait("tensor", s_dw, s_dw.n)
        fin_g[g] = s_mm.n
    p.barrier(s_jn)
    if DEV["debug_outs"]:
        p.dma("gpsimd", dbg["sctx"], sc_sfeat, s_dw)
        p.wait("gpsimd", s_dw)
    if DEV["stop_after"] == "P4":
        p.barrier(s_jn)
        block = es.enter_context(nc.Block())
        p.emit(block)
        es4.close()
        es.close()
        return nc
    es4.close()

    es5 = ExitStack()
    sb5 = lambda name, shape, dt: es5.enter_context(nc.sbuf_tensor(name, shape, dt))
    sf_in = sb5("sf_in", [128, D], F32)
    ga_in = sb5("ga_in", [128, D], F32)
    s_tok = sb5("s_tok", [128, D], F32)
    g_tok = sb5("g_tok", [128, D], F32)
    vm1 = sb5("vm1", [128, D], F32)
    v0_ = sb5("v0_", [128, D], F32)
    vp1 = sb5("vp1", [128, D], F32)
    lep1 = sb5("lep1", [128, D], F32)
    lep2 = sb5("lep2", [128, D], F32)
    x_in = sb5("x_in", [128, D], F32)
    x1t = sb5("x1t", [128, D], F32)
    wlep = sb5("wlep", [1, 4 * D], F32)
    wlb = sb5("wlb", [128, 4 * D], F32)
    lmsk = sb5("lmsk", [128, 2], F32)

    p.dma("sync", lmsk[:], lepe_m, s_ld)
    for t3 in range(3):
        p.dma("sync", wlep[0:1, t3 * D:(t3 + 1) * D], w_lepe[t3:t3 + 1, :], s_ld)
    p.dma("sync", wlep[0:1, 3 * D:4 * D], b_lepe, s_ld)
    ldw = s_ld.n
    p.wait("tensor", s_ld, ldw)
    wlb_cp = {}
    for j in range(8):
        bk = j % 2
        if j >= 2:
            p.wait("tensor", s_ve, wlb_cp[j - 2])
        p.do("tensor", lambda t, j=j, bk=bk: t.matmul(
            out=ps[bk][:], lhsT=ones_c[:], rhs=wlep[0:1, j * 512:(j + 1) * 512],
            start=True, stop=True), inc=s_mm)
        p.wait("vector", s_mm)
        p.do("vector", lambda v, j=j, bk=bk: v.tensor_copy(
            out=wlb[:, j * 512:(j + 1) * 512], in_=ps[bk][:]), inc=s_ve)
        wlb_cp[j] = s_ve.n
    p.wait("vector", s_ld, ldw)

    p5_done = {}
    for i in range(NT_OWN):
        if i >= 1:
            p.wait("sync", s_ve, p5_done.get(i - 1, 0))
        p.dma("sync", sf_in[:].rearrange("p (cb t) -> p cb t", cb=CB),
              sc_sfeat.rearrange("(cb p) t -> p cb t", p=128)[:, :, i * 128:(i + 1) * 128], s_ld)
        p.dma("sync", ga_in[:].rearrange("p (cb t) -> p cb t", cb=CB),
              sc_acc.rearrange("(cb p) t -> p cb t", p=128)[:, :, i * 128:(i + 1) * 128], s_ld)
        base = (i + 1) * 128
        p.dma("sync", vm1[:], sc_vwin[base - 1: base + 127, :], s_ld)
        p.dma("sync", v0_[:], sc_vwin[base: base + 128, :], s_ld)
        p.dma("sync", vp1[:], sc_vwin[base + 1: base + 129, :], s_ld)
        p.dma("sync", x_in[:], xb[i * 128:(i + 1) * 128, :], s_ld)
        ld_i = s_ld.n
        p.wait("scalar", s_ld, ld_i)
        p.do("scalar", lambda s: s.activation(out=ga_in[:], in_=ga_in[:], func=AF.Silu), inc=s_sc)
        sc_i = s_sc.n
        p.wait("tensor", s_ld, ld_i)
        p.wait("tensor", s_sc, sc_i)
        if i >= 1:
            p.wait("tensor", s_ve, p5_done.get(i - 1, 0))
        for j in range(CB):
            p.do("tensor", lambda t, j=j: t.transpose(
                out=ps[j // 4][:, (j % 4) * 128:(j % 4 + 1) * 128],
                in_=sf_in[:, j * 128:(j + 1) * 128], identity=id32[:]),
                inc=(s_tr if j == CB - 1 else None))
        for j in range(CB):
            p.do("tensor", lambda t, j=j: t.transpose(
                out=ps[2 + j // 4][:, (j % 4) * 128:(j % 4 + 1) * 128],
                in_=ga_in[:, j * 128:(j + 1) * 128], identity=id32[:]),
                inc=(s_tr if j == CB - 1 else None))
        tr_i = s_tr.n
        p.wait("vector", s_tr, tr_i)
        p.do("vector", lambda v: v.tensor_copy(out=s_tok[:, 0:512], in_=ps[0][:]))
        p.do("vector", lambda v: v.tensor_copy(out=s_tok[:, 512:1024], in_=ps[1][:]))
        p.do("vector", lambda v: v.tensor_copy(out=g_tok[:, 0:512], in_=ps[2][:]))
        p.do("vector", lambda v: v.tensor_copy(out=g_tok[:, 512:1024], in_=ps[3][:]))
        if i == 0:
            p.do("vector", lambda v: v.tensor_scalar(
                out=vm1[:], in0=vm1[:], scalar1=lmsk[:, 0:1], scalar2=None, op0=ALU.mult))
        if i == NT_OWN - 1:
            p.do("vector", lambda v: v.tensor_scalar(
                out=vp1[:], in0=vp1[:], scalar1=lmsk[:, 1:2], scalar2=None, op0=ALU.mult))
        p.do("vector", lambda v: v.tensor_tensor(out=lep1[:], in0=vm1[:], in1=wlb[:, 0:D], op=ALU.mult))
        p.do("vector", lambda v: v.tensor_tensor(out=lep2[:], in0=v0_[:], in1=wlb[:, D:2 * D], op=ALU.mult))
        p.do("vector", lambda v: v.tensor_tensor(out=lep1[:], in0=lep1[:], in1=lep2[:], op=ALU.add))
        p.do("vector", lambda v: v.tensor_tensor(out=lep2[:], in0=vp1[:], in1=wlb[:, 2 * D:3 * D], op=ALU.mult))
        p.do("vector", lambda v: v.tensor_tensor(out=lep1[:], in0=lep1[:], in1=lep2[:], op=ALU.add))
        p.do("vector", lambda v: v.tensor_tensor(out=lep1[:], in0=lep1[:], in1=wlb[:, 3 * D:4 * D], op=ALU.add))
        p.do("vector", lambda v: v.tensor_tensor(out=s_tok[:], in0=s_tok[:], in1=lep1[:], op=ALU.add))
        p.do("vector", lambda v: v.tensor_tensor(out=s_tok[:], in0=s_tok[:], in1=g_tok[:], op=ALU.mult))
        p.do("vector", lambda v: v.tensor_tensor(out=x1t[:], in0=x_in[:], in1=s_tok[:], op=ALU.add), inc=s_ve)
        p5_done[i] = s_ve.n
        p.wait("gpsimd", s_ve, p5_done[i])
        p.dma("gpsimd", o_x1[i * 128:(i + 1) * 128, :], x1t[:], s_st)
        p.wait("vector", s_st, s_st.n)
    p.barrier(s_jn)
    es5.close()
    if DEV["stop_after"] == "P5":
        block = es.enter_context(nc.Block())
        p.emit(block)
        es.close()
        return nc

    es6 = ExitStack()
    sb6 = lambda name, shape, dt: es6.enter_context(nc.sbuf_tensor(name, shape, dt))
    wq16 = sb6("wq16", [128, CB * D], BF16)
    wk16 = sb6("wk16", [128, CB * D], BF16)
    wv16 = sb6("wv16", [128, CB * D], BF16)
    bq1b = sb6("bq1b", [128, 3 * D], F32)
    x1in = sb6("x1in", [128, D], F32)
    sq6 = sb6("sq6", [128, D], F32)
    st6 = sb6("st6", [128, 12], F32)
    xn6 = sb6("xn6", [128, D], BF16)
    x1nT = sb6("x1nT", [128, CB * TOK], BF16)
    qkvf = sb6("qkvf", [128, 3 * D], F32)
    nrm = sb6("nrm", [128, 4 * NH], F32)
    tmp6a = sb6("tmp6a", [128, 2 * D], F32)
    tmp6b = sb6("tmp6b", [128, 2 * D], F32)
    qn16 = sb6("qn16", [128, D], BF16)
    kn16 = sb6("kn16", [128, D], BF16)
    vv16 = sb6("vv16", [128, D], BF16)
    kvpb = sb6("kvpb", [64, NH * HD], F32)
    evt6 = sb6("evt6", [128, 1024], BF16)

    for wsrc, wdst in ((wq1, wq16), (wk1, wk16), (wv1, wv16)):
        p.dma("gpsimd", wdst[:].rearrange("p (cb o) -> p cb o", cb=CB),
              wsrc.rearrange("(cb p) o -> p cb o", p=128), s_wl)
    p.dma("sync", bq1b[0:1, :], bqkv1, s_ld)
    w6 = s_wl.n
    ld6 = s_ld.n
    p.wait("tensor", s_ld, ld6)
    b6cp = {}
    for j in range(6):
        bk = j % 2
        if j >= 2:
            p.wait("tensor", s_ve, b6cp[j - 2])
        p.do("tensor", lambda t, j=j, bk=bk: t.matmul(
            out=ps[bk][:], lhsT=ones_c[:], rhs=bq1b[0:1, j * 512:(j + 1) * 512],
            start=True, stop=True), inc=s_mm)
        p.wait("vector", s_mm)
        p.do("vector", lambda v, j=j, bk=bk: v.tensor_copy(
            out=bq1b[:, j * 512:(j + 1) * 512], in_=ps[bk][:]), inc=s_ve)
        b6cp[j] = s_ve.n
    p.wait("tensor", s_wl, w6)

    p6_fin = {}
    for i in range(NT_OWN):
        if i >= 1:
            p.wait("sync", s_ve, p6_fin.get(i - 1, 0))
        p.dma("sync", x1in[:], o_x1[i * 128:(i + 1) * 128, :], s_ld)
        ld_i = s_ld.n
        p.wait("scalar", s_ld, ld_i)
        p.do("scalar", lambda s: s.activation(out=sq6[:], in_=x1in[:], func=AF.Square), inc=s_sc)
        sc_sq = s_sc.n
        p.wait("vector", s_ld, ld_i)
        p.do("vector", lambda v: v.reduce_sum(out=st6[:, 0:1], in_=x1in[:], axis=AX.X))
        p.wait("vector", s_sc, sc_sq)
        p.do("vector", lambda v: v.reduce_sum(out=st6[:, 1:2], in_=sq6[:], axis=AX.X))
        p.do("vector", lambda v: v.tensor_scalar(out=st6[:, 2:3], in0=st6[:, 0:1], scalar1=1.0 / D, scalar2=None, op0=ALU.mult))
        p.do("vector", lambda v: v.tensor_scalar(out=st6[:, 3:4], in0=st6[:, 1:2], scalar1=1.0 / D, scalar2=None, op0=ALU.mult))
        p.do("vector", lambda v: v.tensor_tensor(out=st6[:, 8:9], in0=st6[:, 2:3], in1=st6[:, 2:3], op=ALU.mult))
        p.do("vector", lambda v: v.tensor_tensor(out=st6[:, 9:10], in0=st6[:, 3:4], in1=st6[:, 8:9], op=ALU.subtract))
        p.do("vector", lambda v: v.tensor_scalar(out=st6[:, 4:5], in0=st6[:, 9:10], scalar1=LN_EPS, scalar2=None, op0=ALU.add), inc=s_ve)
        p.wait("scalar", s_ve)
        p.do("scalar", lambda s: s.activation(out=st6[:, 5:6], in_=st6[:, 4:5], func=AF.Sqrt), inc=s_sc)
        p.wait("vector", s_sc)
        p.do("vector", lambda v: v.reciprocal(out=st6[:, 6:7], in_=st6[:, 5:6]))
        p.do("vector", lambda v: v.tensor_scalar(out=st6[:, 7:8], in0=st6[:, 6:7], scalar1=st6[:, 2:3], scalar2=-1.0, op0=ALU.mult, op1=ALU.mult))
        p.do("vector", lambda v: v.tensor_scalar(out=xn6[:], in0=x1in[:], scalar1=st6[:, 6:7], scalar2=st6[:, 7:8], op0=ALU.mult, op1=ALU.add), inc=s_ve)
        xn_n = s_ve.n
        p.wait("tensor", s_ve, xn_n)
        if i >= 1:
            p.wait("tensor", s_ve, p6_fin.get(("xevac", i - 1), 0))
        for j in range(CB):
            p.do("tensor", lambda t, j=j: t.transpose(
                out=pt16[0][:, j * 128:(j + 1) * 128], in_=xn6[:, j * 128:(j + 1) * 128],
                identity=id16[:]), inc=(s_tr if j == CB - 1 else None))
        tr_x = s_tr.n
        p.wait("vector", s_tr, tr_x)
        p.do("vector", lambda v, i=i: v.tensor_copy(
            out=x1nT[:].rearrange("p (cb t) -> p cb t", cb=CB)[:, :, i * 128:(i + 1) * 128],
            in_=pt16[0][:].rearrange("p (cb t) -> p cb t", cb=CB)), inc=s_ve)
        p6_fin[("xevac", i)] = s_ve.n
        xev = s_ve.n
        p.wait("tensor", s_ve, xev)
        for blk in range(6):
            wsel = (wq16, wk16, wv16)[blk // 2]
            ocol = (blk % 2) * 512
            bk = blk % 2
            if blk >= 2:
                p.wait("tensor", s_ve, p6_fin.get(("pe", i, blk - 2), 0))
            for cb in range(CB):
                p.do("tensor", lambda t, i=i, cb=cb, wsel=wsel, ocol=ocol, bk=bk: t.matmul(
                    out=ps[bk][:],
                    lhsT=x1nT[:].rearrange("p (cb t) -> p cb t", cb=CB)[:, cb, i * 128:(i + 1) * 128],
                    rhs=wsel[:, cb * D + ocol: cb * D + ocol + 512],
                    start=(cb == 0), stop=(cb == CB - 1)),
                    inc=(s_mm if cb == CB - 1 else None))
            mmn = s_mm.n
            p.wait("vector", s_mm, mmn)
            p.do("vector", lambda v, blk=blk, bk=bk: v.tensor_tensor(
                out=qkvf[:, blk * 512:(blk + 1) * 512], in0=ps[bk][:],
                in1=bq1b[:, blk * 512:(blk + 1) * 512], op=ALU.add), inc=s_ve)
            p6_fin[("pe", i, blk)] = s_ve.n
        QK = qkvf[:, 0:2 * D]
        p.do("vector", lambda v: v.tensor_scalar(out=tmp6a[:], in0=QK, scalar1=0.0, scalar2=None, op0=ALU.min), inc=s_ve)
        p.wait("scalar", s_ve)
        p.do("scalar", lambda s: s.activation(out=tmp6b[:], in_=tmp6a[:], func=AF.Exp), inc=s_sc)
        p.wait("vector", s_sc)
        p.do("vector", lambda v: v.tensor_scalar(out=tmp6a[:], in0=QK, scalar1=0.0, scalar2=None, op0=ALU.max))
        p.do("vector", lambda v: v.tensor_tensor(out=tmp6a[:], in0=tmp6a[:], in1=tmp6b[:], op=ALU.add))
        p.do("vector", lambda v: v.tensor_scalar(out=tmp6a[:], in0=tmp6a[:], scalar1=-1.0, scalar2=None, op0=ALU.add), inc=s_ve)
        elu_n = s_ve.n
        p.wait("scalar", s_ve, elu_n)
        p.do("scalar", lambda s: s.activation(out=tmp6b[:], in_=tmp6a[:], func=AF.Square), inc=s_sc)
        p.wait("vector", s_sc)
        for h in range(2 * NH):
            p.do("vector", lambda v, h=h: v.reduce_sum(
                out=nrm[:, h:h + 1], in_=tmp6b[:, h * 64:(h + 1) * 64], axis=AX.X),
                inc=(s_ve if h == 2 * NH - 1 else None))
        p.wait("scalar", s_ve)
        p.do("scalar", lambda s: s.activation(out=nrm[:, 32:64], in_=nrm[:, 0:32], func=AF.Sqrt), inc=s_sc)
        p.wait("vector", s_sc)
        p.do("vector", lambda v: v.reciprocal(out=nrm[:, 0:32], in_=nrm[:, 32:64]), inc=s_ve)
        for h in range(NH):
            p.do("vector", lambda v, h=h: v.tensor_scalar(
                out=qn16[:, h * 64:(h + 1) * 64], in0=tmp6a[:, h * 64:(h + 1) * 64],
                scalar1=nrm[:, h:h + 1], scalar2=None, op0=ALU.mult))
        for h in range(NH):
            p.do("vector", lambda v, h=h: v.tensor_scalar(
                out=kn16[:, h * 64:(h + 1) * 64], in0=tmp6a[:, D + h * 64:D + (h + 1) * 64],
                scalar1=nrm[:, NH + h:NH + h + 1], scalar2=None, op0=ALU.mult))
        p.do("vector", lambda v: v.tensor_copy(out=vv16[:], in_=qkvf[:, 2 * D:3 * D]), inc=s_ve)
        qkv_n = s_ve.n
        p.wait("tensor", s_ve, qkv_n)
        for h in range(NH):
            p.do("tensor", lambda t, h=h, i=i: t.matmul(
                out=ps[2 + h // 8][0:64, (h % 8) * 64:(h % 8) * 64 + 64],
                lhsT=kn16[:, h * 64:(h + 1) * 64], rhs=vv16[:, h * 64:(h + 1) * 64],
                start=(i == 0 and h % 8 == 0), stop=(i == NT_OWN - 1), skip_group_check=True),
                inc=(s_mm if h == NH - 1 else None))
        if i >= 1:
            p.wait("tensor", s_dw, p6_fin.get(("qdw", i - 1), 0))
        for j in range(CB):
            p.do("tensor", lambda t, j=j: t.transpose(
                out=pt16[1][:, j * 128:(j + 1) * 128], in_=qn16[:, j * 128:(j + 1) * 128],
                identity=id16[:]), inc=(s_tr if j == CB - 1 else None))
        tr_q = s_tr.n
        p.wait("vector", s_tr, tr_q)
        p.do("vector", lambda v: v.tensor_copy(out=evt6[:, 0:512], in_=pt16[1][:, 0:512]))
        p.do("vector", lambda v: v.tensor_copy(out=evt6[:, 512:1024], in_=pt16[1][:, 512:1024]), inc=s_ve)
        qev = s_ve.n
        p.wait("gpsimd", s_ve, qev)
        p.dma("gpsimd", o_qnT.rearrange("(cb p) t -> p cb t", p=128)[:, :, i * 128:(i + 1) * 128],
              evt6[:].rearrange("p (cb t) -> p cb t", cb=CB), s_st)
        p6_fin[("qdw", i)] = s_st.n
        p.wait("vector", s_st, s_st.n)
        p6_fin[i] = s_ve.n
    p.wait("gpsimd", s_ve, s_ve.n)
    p.dma("gpsimd", o_x1nT.rearrange("(cb p) t -> p cb t", p=128),
          x1nT[:].rearrange("p (cb t) -> p cb t", cb=CB), s_st)
    kv_mm = s_mm.n
    p.wait("vector", s_mm, kv_mm)
    p.do("vector", lambda v: v.tensor_copy(out=kvpb[:, 0:512], in_=ps[2][0:64, :]))
    p.do("vector", lambda v: v.tensor_copy(out=kvpb[:, 512:1024], in_=ps[3][0:64, :]), inc=s_ve)
    p.wait("gpsimd", s_ve, s_ve.n)
    p.dma("gpsimd", o_kvp, kvpb[:], s_st)

    for e in ENGS:
        p.wait(e, s_st)
    p.barrier(s_jn)
    block = es.enter_context(nc.Block())
    p.emit(block)
    es6.close()
    es.close()
    return nc



def build_launch_b():
    nc = bass.Bass("TRN2", target_bir_lowering=False)
    es = ExitStack()
    dram_in = lambda n, shp, dt=F32: nc.dram_tensor(n, shp, dt, kind="ExternalInput").ap()
    dram_out = lambda n, shp, dt=F32: nc.dram_tensor(n, shp, dt, kind="ExternalOutput").ap()
    dram_tmp = lambda n, shp, dt=F32: nc.dram_tensor(n, shp, dt, kind="Internal").ap()

    x1 = dram_in("x1", [TOK, D])
    x1nT_in = dram_in("x1nT_in", [D, TOK], BF16)
    qnT_in = dram_in("qnT_in", [D, TOK], BF16)
    kvp4 = dram_in("kvp4", [4, HD, NH * HD])
    wo = dram_in("wo", [D, D])
    w1 = dram_in("w1", [D, FF])
    w2 = dram_in("w2", [FF, D])
    fw1T = dram_in("fw1T", [D, FF])
    fw2T = dram_in("fw2T", [FF, D])
    bo_c = dram_in("bo_c", [128, CB])
    b1_c = dram_in("b1_c", [128, FF // 128])
    b2_c = dram_in("b2_c", [128, CB])
    fb1_c = dram_in("fb1_c", [128, FF // 128])
    fb2_c = dram_in("fb2_c", [128, CB])
    g1_c = dram_in("g1_c", [128, CB])
    be1_c = dram_in("be1_c", [128, CB])
    g2_c = dram_in("g2_c", [128, CB])
    be2_c = dram_in("be2_c", [128, CB])
    ident32 = dram_in("ident32", [128, 128])
    ones_r = dram_in("ones_r", [1, 128])
    ones_cl = dram_in("ones_cl", [128, 1])

    o_x2 = dram_out("o_x2", [TOK, D])
    NFB = FF // 128

    dbg = {}
    if DEV["debug_outs"]:
        dbg["ao"] = dram_out("dbg_ao", [D, TOK])
        dbg["h0"] = dram_out("dbg_h0", [NFB, 128, 512], BF16)
        dbg["z2"] = dram_out("dbg_z2", [CB, 128, 512])
        dbg["o7"] = dram_out("dbg_o7", [128, 512])
        dbg["hq0"] = dram_out("dbg_hq0", [NFB, 128, 512], BF16)
        dbg["prod0"] = dram_out("dbg_prod0", [CB, 128, 512])
        dbg["wq1s"] = dram_out("dbg_wq1s", [128, FF], BF16)
        dbg["schq"] = dram_out("dbg_schq", [2, 128, TOK], BF16)
        dbg["rows"] = dram_out("dbg_rows", [2, 512])
        dbg["aoc7"] = dram_out("dbg_aoc7", [128, 512])
        dbg["s1"] = dram_out("dbg_s1", [D, TOK])
        dbg["g1s"] = dram_out("dbg_g1s", [D, TOK])

    sc_wq1 = dram_tmp("sc_wq1", [CB, 128, FF], BF16)
    sc_wq2 = dram_tmp("sc_wq2", [FF // 128, 128, D], BF16)
    sc_h = dram_tmp("sc_h", [FF // 128, 128, TOK], BF16)
    sc_hq = dram_tmp("sc_hq", [FF // 128, 128, TOK], BF16)
    sc_ao = dram_tmp("sc_ao", [CB, 128, TOK])
    sc_s1 = dram_tmp("sc_s1", [CB, 128, TOK])

    sb = lambda name, shape, dt: es.enter_context(nc.sbuf_tensor(name, shape, dt))
    x1nT = sb("x1nT", [128, CB * TOK], BF16)
    ones_c = sb("ones_c", [1, 128], F32)
    ones_col = sb("ones_col", [128, 1], F32)
    id32 = sb("id32", [128, 128], F32)
    bcols = sb("bcols", [128, 7 * CB + 2 * NFB], F32)
    srow = sb("srow", [1, TOK], F32)
    arow = sb("arow", [1, TOK], F32)
    brow = sb("brow", [1, TOK], F32)
    stg = sb("stg", [128, 512], F32)
    stg2 = sb("stg2", [128, 512], F32)
    stg16 = sb("stg16", [128, 512], BF16)
    gcol = sb("gcol", [128, 4], F32)

    ps = [es.enter_context(nc.psum_tensor(f"ps{i}", [128, 512], F32)) for i in range(6)]
    pt32 = es.enter_context(nc.psum_tensor("pt32", [128, 1024], F32))

    es_ao = ExitStack()
    aof = es_ao.enter_context(nc.sbuf_tensor("aof", [128, CB * TOK], F32))

    p = Prog(nc, es)
    s_ld = p.sem("s_ld"); s_wl = p.sem("s_wl"); s_sc = p.sem("s_sc")
    s_ve = p.sem("s_ve"); s_mm = p.sem("s_mm"); s_tr = p.sem("s_tr")
    s_dw = p.sem("s_dw"); s_st = p.sem("s_st"); s_jn = p.sem("s_jn")

    p.dma("sync", ones_c[:], ones_r, s_ld)
    p.dma("sync", ones_col[:], ones_cl, s_ld)
    p.dma("sync", id32[:], ident32, s_ld)
    off = {}
    o = 0
    for nm, ap_, w in (("bo", bo_c, CB), ("b2", b2_c, CB), ("g1", g1_c, CB), ("be1", be1_c, CB),
                       ("g2", g2_c, CB), ("be2", be2_c, CB), ("fb2", fb2_c, CB),
                       ("b1", b1_c, NFB), ("fb1", fb1_c, NFB)):
        p.dma("sync", bcols[:, o:o + w], ap_, s_ld)
        off[nm] = o
        o += w
    p.dma("gpsimd", x1nT[:].rearrange("p (cb t) -> p cb t", cb=CB),
          x1nT_in.rearrange("(cb p) t -> p cb t", p=128), s_wl)
    const_ld = s_ld.n
    p.wait("vector", s_ld, const_ld)
    p.wait("tensor", s_ld, const_ld)

    es_q = ExitStack()
    sbq = lambda name, shape, dt: es_q.enter_context(nc.sbuf_tensor(name, shape, dt))
    qw = [sbq(f"qw{i}", [128, FF], F32) for i in range(2)]
    qacc = sbq("qacc", [128, 4], F32)
    qout = sbq("qout", [128, FF], BF16)
    pass_guard = 0
    for wi, (wsrc, scdst, nch, chw) in enumerate(((fw1T, sc_wq1, CB, FF), (fw2T, sc_wq2, NFB, D))):
        p.do("vector", lambda v: v.memset(qacc[:, 0:1], 0.0), inc=s_ve)
        red_n = {}
        for cb in range(nch):
            p.wait("sync", s_ve, red_n.get(cb - 2, pass_guard))
            p.dma("sync", qw[cb % 2][:, 0:chw], wsrc[cb * 128:(cb + 1) * 128, :], s_ld)
            p.wait("vector", s_ld)
            p.do("vector", lambda v, cb=cb, chw=chw: v.reduce_sum(
                out=qacc[:, 1:2], in_=qw[cb % 2][:, 0:chw], axis=AX.X, apply_absolute_value=True))
            p.do("vector", lambda v: v.tensor_tensor(
                out=qacc[:, 0:1], in0=qacc[:, 0:1], in1=qacc[:, 1:2], op=ALU.add), inc=s_ve)
            red_n[cb] = s_ve.n
        tot = s_ve.n
        p.wait("tensor", s_ve, tot)
        p.do("tensor", lambda t: t.matmul(out=ps[0][0:1, 0:1], lhsT=ones_col[:], rhs=qacc[:, 0:1],
                                          start=True, stop=True), inc=s_mm)
        p.wait("vector", s_mm)
        nelem = nch * 128 * chw
        p.do("vector", lambda v, nelem=nelem: v.tensor_scalar(
            out=qacc[0:1, 2:3], in0=ps[0][0:1, 0:1], scalar1=1.0 / nelem, scalar2=None, op0=ALU.mult))
        p.do("vector", lambda v: v.tensor_scalar(
            out=qacc[0:1, 2:3], in0=qacc[0:1, 2:3], scalar1=QEPS, scalar2=0.5,
            op0=ALU.add, op1=ALU.mult), inc=s_ve)
        p.wait("tensor", s_ve)
        p.do("tensor", lambda t: t.matmul(out=ps[1][:, 0:1], lhsT=ones_c[:], rhs=qacc[0:1, 2:3],
                                          start=True, stop=True), inc=s_mm)
        p.wait("vector", s_mm)
        p.do("vector", lambda v, wi=wi: v.tensor_copy(out=gcol[:, 2 * wi:2 * wi + 1], in_=ps[1][:, 0:1]))
        p.do("vector", lambda v, wi=wi: v.tensor_scalar(
            out=gcol[:, 2 * wi + 1:2 * wi + 2], in0=gcol[:, 2 * wi:2 * wi + 1],
            scalar1=-1.0, scalar2=None, op0=ALU.mult), inc=s_ve)
        qz_n = {}
        pass_guard = s_ve.n
        for cb in range(nch):
            p.wait("sync", s_ve, qz_n.get(cb - 2, pass_guard))
            p.dma("sync", qw[cb % 2][:, 0:chw], wsrc[cb * 128:(cb + 1) * 128, :], s_ld)
            p.wait("vector", s_ld)
            p.do("vector", lambda v, cb=cb, chw=chw, wi=wi: v.tensor_scalar(
                out=qout[:, 0:chw], in0=qw[cb % 2][:, 0:chw],
                scalar1=gcol[:, 2 * wi:2 * wi + 1], scalar2=None, op0=ALU.is_gt))
            p.do("vector", lambda v, cb=cb, chw=chw, wi=wi: v.tensor_scalar(
                out=qw[cb % 2][:, 0:chw], in0=qw[cb % 2][:, 0:chw],
                scalar1=gcol[:, 2 * wi + 1:2 * wi + 2], scalar2=None, op0=ALU.is_lt))
            p.do("vector", lambda v, cb=cb, chw=chw: v.tensor_tensor(
                out=qout[:, 0:chw], in0=qout[:, 0:chw], in1=qw[cb % 2][:, 0:chw], op=ALU.subtract), inc=s_ve)
            qn_ = s_ve.n
            qz_n[cb] = qn_
            p.wait("gpsimd", s_ve, qn_)
            p.dma("gpsimd", scdst[cb], qout[:, 0:chw], s_dw)
            p.wait("vector", s_dw, s_dw.n)
        pass_guard = s_ve.n
    if DEV["debug_outs"]:
        p.dma("gpsimd", dbg["wq1s"], sc_wq1[6], s_dw)
        p.wait("gpsimd", s_dw)
    p.barrier(s_jn)
    es_q.close()

    es_c = ExitStack()
    sbc = lambda name, shape, dt: es_c.enter_context(nc.sbuf_tensor(name, shape, dt))
    kvl = sbc("kvl", [64, 2 * NH * HD], F32)
    kv16 = sbc("kv16", [64, NH * HD], BF16)
    qnTc = [sbc(f"qnTc{i}", [64, TOK], BF16) for i in range(2)]
    ctxT = sbc("ctxT", [128, CB * TOK], BF16)
    p.dma("sync", kvl[:].rearrange("p (r c) -> p r c", r=2), kvp4[0:2].rearrange("r p c -> p r c"), s_ld)
    qn_ld = s_wl.n
    p.wait("vector", s_ld)
    p.do("vector", lambda v: v.tensor_tensor(out=kvl[:, 0:1024], in0=kvl[:, 0:1024], in1=kvl[:, 1024:2048], op=ALU.add), inc=s_ve)
    p.wait("sync", s_ve)
    p.dma("sync", kvl[:, 1024:2048], kvp4[2], s_ld)
    p.wait("vector", s_ld)
    p.do("vector", lambda v: v.tensor_tensor(out=kvl[:, 0:1024], in0=kvl[:, 0:1024], in1=kvl[:, 1024:2048], op=ALU.add), inc=s_ve)
    p.wait("sync", s_ve)
    p.dma("sync", kvl[:, 1024:2048], kvp4[3], s_ld)
    p.wait("vector", s_ld)
    p.do("vector", lambda v: v.tensor_tensor(out=kvl[:, 0:1024], in0=kvl[:, 0:1024], in1=kvl[:, 1024:2048], op=ALU.add))
    p.do("vector", lambda v: v.tensor_scalar(out=kv16[:], in0=kvl[:, 0:1024], scalar1=0.125, scalar2=None, op0=ALU.mult), inc=s_ve)
    kv_n = s_ve.n
    p.wait("tensor", s_ve, kv_n)
    ctx_ev = {}
    qc_ld = {}
    for h in range(NH):
        if h >= 2:
            p.wait("gpsimd", s_mm, ctx_ev.get(("mm", h - 2), 0))
        p.dma("gpsimd", qnTc[h % 2][:], qnT_in[h * 64:(h + 1) * 64, :], s_wl)
        qc_ld[h] = s_wl.n
        p.wait("tensor", s_wl, qc_ld[h])
        for tb in range(4):
            u = h * 4 + tb
            bank = u % 2
            if u >= 2:
                p.wait("tensor", s_ve, ctx_ev.get(u - 2, 0))
            p.do("tensor", lambda t, h=h, tb=tb, bank=bank: t.matmul(
                out=ps[bank][0:64, :],
                lhsT=kv16[:, h * 64:(h + 1) * 64],
                rhs=qnTc[h % 2][:, tb * 512:(tb + 1) * 512],
                start=True, stop=True), inc=s_mm)
            p.wait("vector", s_mm)
            p.do("vector", lambda v, h=h, tb=tb, bank=bank: v.tensor_copy(
                out=ctxT[:].rearrange("p (c t) -> p c t", c=CB)[(h % 2) * 64:(h % 2) * 64 + 64, h // 2, tb * 512:(tb + 1) * 512],
                in_=ps[bank][0:64, :]), inc=s_ve)
            ctx_ev[u] = s_ve.n
        ctx_ev[("mm", h)] = s_mm.n
    p.barrier(s_jn)

    wo16 = sbc("wo16", [128, CB * D], BF16)
    p.dma("gpsimd", wo16[:].rearrange("p (cb o) -> p cb o", cb=CB),
          wo.rearrange("(cb p) o -> p cb o", p=128), s_wl)
    wo_ld = s_wl.n
    p.wait("tensor", s_wl, wo_ld)
    z_ev = {}
    for co in range(CB):
        for tb in range(4):
            u = co * 4 + tb
            bank = u % 2
            if u >= 2:
                p.wait("tensor", s_ve, z_ev.get(u - 2, 0))
            for ce in range(CB):
                p.do("tensor", lambda t, co=co, tb=tb, ce=ce, bank=bank: t.matmul(
                    out=ps[bank][:],
                    lhsT=wo16[:, ce * D + co * 128: ce * D + (co + 1) * 128],
                    rhs=ctxT[:, ce * TOK + tb * 512: ce * TOK + tb * 512 + 512],
                    start=(ce == 0), stop=(ce == CB - 1)),
                    inc=(s_mm if ce == CB - 1 else None))
            p.wait("vector", s_mm)
            sl = slice(co * TOK + tb * 512, co * TOK + tb * 512 + 512)
            p.do("vector", lambda v, sl=sl: v.tensor_copy(out=stg[:], in_=x1nT[:, sl]))
            p.do("vector", lambda v, co=co, bank=bank: v.tensor_scalar(
                out=stg2[:], in0=ps[bank][:], scalar1=bcols[:, off["bo"] + co:off["bo"] + co + 1],
                scalar2=None, op0=ALU.add))
            p.do("vector", lambda v, sl=sl: v.tensor_tensor(
                out=aof[:, sl], in0=stg2[:], in1=stg[:], op=ALU.add), inc=s_ve)
            z_ev[u] = s_ve.n
    p.barrier(s_jn)

    def feat_ln_affine(buf, tok_w, gname, bname, eps, dump_rows=False):
        ntb = tok_w // 512
        p.wait("tensor", s_ve)
        p.wait("scalar", s_ve)
        stats_cp = {}
        for tb in range(ntb):
            if tb >= 1:
                p.wait("tensor", s_ve, stats_cp[tb - 1])
            sq_n = {}
            for cb in range(CB):
                sl = slice(cb * tok_w + tb * 512, cb * tok_w + tb * 512 + 512)
                p.do("tensor", lambda t, sl=sl, cb=cb: t.matmul(
                    out=ps[4][0:1, :], lhsT=ones_col[:], rhs=buf[:, sl],
                    start=(cb == 0), stop=(cb == CB - 1), skip_group_check=True), inc=s_mm)
            for cb in range(CB):
                sl = slice(cb * tok_w + tb * 512, cb * tok_w + tb * 512 + 512)
                if cb >= 1:
                    p.wait("scalar", s_mm, sq_n[cb - 1])
                p.do("scalar", lambda s, sl=sl: s.activation(out=stg[:], in_=buf[:, sl], func=AF.Square), inc=s_sc)
                p.wait("tensor", s_sc)
                p.do("tensor", lambda t, cb=cb: t.matmul(
                    out=ps[5][0:1, :], lhsT=ones_col[:], rhs=stg[:],
                    start=(cb == 0), stop=(cb == CB - 1), skip_group_check=True), inc=s_mm)
                sq_n[cb] = s_mm.n
            mm_st = s_mm.n
            p.wait("vector", s_mm, mm_st)
            p.do("vector", lambda v, tb=tb: v.tensor_scalar(
                out=arow[0:1, tb * 512:(tb + 1) * 512], in0=ps[4][0:1, :], scalar1=1.0 / D, scalar2=None, op0=ALU.mult))
            p.do("vector", lambda v, tb=tb: v.tensor_scalar(
                out=brow[0:1, tb * 512:(tb + 1) * 512], in0=ps[5][0:1, :], scalar1=1.0 / D, scalar2=None, op0=ALU.mult), inc=s_ve)
            stats_cp[tb] = s_ve.n
        W = tok_w
        p.do("vector", lambda v: v.tensor_tensor(out=srow[0:1, 0:W], in0=arow[0:1, 0:W], in1=arow[0:1, 0:W], op=ALU.mult))
        p.do("vector", lambda v: v.tensor_tensor(out=brow[0:1, 0:W], in0=brow[0:1, 0:W], in1=srow[0:1, 0:W], op=ALU.subtract))
        p.do("vector", lambda v, eps=eps: v.tensor_scalar(out=brow[0:1, 0:W], in0=brow[0:1, 0:W], scalar1=eps, scalar2=None, op0=ALU.add), inc=s_ve)
        p.wait("scalar", s_ve)
        p.do("scalar", lambda s: s.activation(out=srow[0:1, 0:W], in_=brow[0:1, 0:W], func=AF.Sqrt), inc=s_sc)
        p.wait("vector", s_sc)
        p.do("vector", lambda v: v.reciprocal(out=brow[0:1, 0:W], in_=srow[0:1, 0:W]))
        p.do("vector", lambda v: v.tensor_tensor(out=arow[0:1, 0:W], in0=arow[0:1, 0:W], in1=brow[0:1, 0:W], op=ALU.mult))
        p.do("vector", lambda v: v.tensor_scalar(out=arow[0:1, 0:W], in0=arow[0:1, 0:W], scalar1=-1.0, scalar2=None, op0=ALU.mult), inc=s_ve)
        rows_n = s_ve.n
        if DEV["debug_outs"] and dump_rows:
            p.wait("gpsimd", s_ve, rows_n)
            p.dma("gpsimd", dbg["rows"][0:1, :], arow[0:1, 0:512], s_dw)
            p.dma("gpsimd", dbg["rows"][1:2, :], brow[0:1, 0:512], s_dw)
            p.wait("vector", s_dw)
            p.wait("tensor", s_dw)
        p.wait("tensor", s_ve, rows_n)
        ap_ev = {}
        for tb in range(ntb):
            bb = tb % 2
            if tb >= 2:
                p.wait("tensor", s_ve, ap_ev[tb - 2])
            p.do("tensor", lambda t, tb=tb, bb=bb: t.matmul(
                out=ps[bb][:], lhsT=ones_c[:], rhs=brow[0:1, tb * 512:(tb + 1) * 512],
                start=True, stop=True), inc=s_mm)
            p.do("tensor", lambda t, tb=tb, bb=bb: t.matmul(
                out=ps[2 + bb][:], lhsT=ones_c[:], rhs=arow[0:1, tb * 512:(tb + 1) * 512],
                start=True, stop=True), inc=s_mm)
            bc_n = s_mm.n
            p.wait("vector", s_mm, bc_n)
            for cb in range(CB):
                sl = slice(cb * tok_w + tb * 512, cb * tok_w + tb * 512 + 512)
                p.do("vector", lambda v, sl=sl, bb=bb: v.tensor_tensor(
                    out=stg[:], in0=buf[:, sl], in1=ps[bb][:], op=ALU.mult))
                p.do("vector", lambda v, bb=bb: v.tensor_tensor(
                    out=stg[:], in0=stg[:], in1=ps[2 + bb][:], op=ALU.add))
                p.do("vector", lambda v, sl=sl, cb=cb: v.tensor_scalar(
                    out=buf[:, sl], in0=stg[:],
                    scalar1=bcols[:, off[gname] + cb:off[gname] + cb + 1],
                    scalar2=bcols[:, off[bname] + cb:off[bname] + cb + 1],
                    op0=ALU.mult, op1=ALU.add), inc=s_ve)
            ap_ev[tb] = s_ve.n
        return s_ve.n

    feat_ln_affine(aof, TOK, "g1", "be1", L1_EPS)
    p.barrier(s_jn)
    if DEV["debug_outs"]:
        p.dma("gpsimd", dbg["ao"].rearrange("(cb p) t -> p cb t", p=128),
              aof[:].rearrange("p (cb t) -> p cb t", cb=CB), s_dw)
        p.wait("gpsimd", s_dw)
    if DEV["stop_after"] == "B4":
        p.barrier(s_jn)
        block = es.enter_context(nc.Block())
        p.emit(block)
        es_c.close(); es.close()
        return nc

    ao_pg = {}
    for cb in range(CB):
        p.dma("gpsimd", sc_ao[cb], aof[:].rearrange("p (c t) -> p c t", c=CB)[:, cb, :], s_dw)
    p.wait("gpsimd", s_dw)
    p.barrier(s_jn)
    es_c.close()

    es_f1 = ExitStack()
    sbf1 = lambda name, shape, dt: es_f1.enter_context(nc.sbuf_tensor(name, shape, dt))
    ao16 = sbf1("ao16", [128, CB * TOK], BF16)
    w1b = [sbf1(f"w1b{i}", [128, CB * 128], BF16) for i in range(2)]
    hstg16 = [sbf1(f"hstg16_{i}", [128, 512], BF16) for i in range(2)]
    for cb in range(CB):
        for tb in range(4):
            sl = slice(cb * TOK + tb * 512, cb * TOK + tb * 512 + 512)
            p.do("vector", lambda v, sl=sl: v.tensor_copy(out=ao16[:, sl], in_=aof[:, sl]),
                 inc=(s_ve if (cb, tb) == (CB - 1, 3) else None))
    ao16_n = s_ve.n
    p.wait("tensor", s_ve, ao16_n)
    hb_dw = {}
    for fb in range(NFB):
        wb = fb % 2
        if fb >= 2:
            p.wait("gpsimd", s_mm, hb_dw.get(("mm", fb - 2), 0))
        p.dma("gpsimd", w1b[wb][:].rearrange("p (cb f) -> p cb f", cb=CB),
              w1[:, fb * 128:(fb + 1) * 128].rearrange("(cb p) f -> p cb f", p=128), s_wl)
        p.wait("tensor", s_wl)
        for tb in range(4):
            u = fb * 4 + tb
            bank = u % 2
            if u >= 2:
                p.wait("tensor", s_sc, hb_dw.get(("ev", u - 2), 0))
            for cb in range(CB):
                p.do("tensor", lambda t, cb=cb, tb=tb, wb=wb, bank=bank: t.matmul(
                    out=ps[bank][:], lhsT=w1b[wb][:, cb * 128:(cb + 1) * 128],
                    rhs=ao16[:, cb * TOK + tb * 512: cb * TOK + tb * 512 + 512],
                    start=(cb == 0), stop=(cb == CB - 1)),
                    inc=(s_mm if cb == CB - 1 else None))
            if tb == 3:
                hb_dw[("mm", fb)] = s_mm.n
            mm_h = s_mm.n
            p.wait("scalar", s_mm, mm_h)
            if u >= 2:
                p.wait("scalar", s_dw, hb_dw.get(("dw", u - 2), 0))
            p.do("scalar", lambda s, fb=fb, bank=bank, u=u: s.activation(
                out=hstg16[u % 2][:], in_=ps[bank][:], func=AF.Gelu,
                bias=bcols[:, off["b1"] + fb:off["b1"] + fb + 1]), inc=s_sc)
            hb_dw[("ev", u)] = s_sc.n
            p.wait("gpsimd", s_sc, s_sc.n)
            p.dma("gpsimd", sc_h[fb, :, tb * 512:(tb + 1) * 512], hstg16[u % 2][:], s_dw)
            hb_dw[("dw", u)] = s_dw.n
    p.barrier(s_jn)
    es_f1.close()
    es_ao.close()

    es_f2 = ExitStack()
    sbf2 = lambda name, shape, dt: es_f2.enter_context(nc.sbuf_tensor(name, shape, dt))
    w2r = sbf2("w2r", [128, NFB * D], BF16)
    htb = [sbf2(f"htb{i}", [128, NFB * 512], BF16) for i in range(1)]
    z2 = sbf2("z2", [128, CB * 512], F32)
    aoc = sbf2("aoc", [128, 512], F32)
    for half in range(4):
        p.dma("gpsimd", w2r[:].rearrange("p (fb o) -> p fb o", fb=NFB)[:, half * 8:(half + 1) * 8, :],
              w2[half * 8 * 128:(half + 1) * 8 * 128, :].rearrange("(fb p) o -> p fb o", p=128), s_wl)
    w2_ld = s_wl.n
    p.wait("tensor", s_wl, w2_ld)
    ln_guard = [0]
    aoc_guard = [0]
    for tb in range(4):
        p.wait("gpsimd", s_mm if tb > 0 else s_wl)
        for half in range(4):
            p.dma("gpsimd", htb[0][:].rearrange("p (fb t) -> p fb t", fb=NFB)[:, half * 8:(half + 1) * 8, :],
                  sc_h[half * 8:(half + 1) * 8, :, tb * 512:(tb + 1) * 512].rearrange("fb p t -> p fb t"), s_wl)
        h_ld = s_wl.n
        p.wait("tensor", s_wl, h_ld)
        z_cp = {}
        for co in range(CB):
            bank = co % 2
            if co < 2:
                p.wait("tensor", s_ve, ln_guard[0])
            if co >= 2:
                p.wait("tensor", s_ve, z_cp[co - 2])
            for fb in range(NFB):
                p.do("tensor", lambda t, co=co, fb=fb, bank=bank: t.matmul(
                    out=ps[bank][:], lhsT=w2r[:, fb * D + co * 128: fb * D + (co + 1) * 128],
                    rhs=htb[0][:, fb * 512:(fb + 1) * 512],
                    start=(fb == 0), stop=(fb == NFB - 1)),
                    inc=(s_mm if fb == NFB - 1 else None))
            mm_z = s_mm.n
            p.wait("sync", s_ve, z_cp.get(co - 1, aoc_guard[0]))
            p.dma("sync", aoc[:], sc_ao[co, :, tb * 512:(tb + 1) * 512], s_ld)
            ao_ld = s_ld.n
            p.wait("vector", s_mm, mm_z)
            p.wait("vector", s_ld, ao_ld)
            if tb == 0 and co == 7 and DEV["debug_outs"]:
                p.do("vector", lambda v, bank=bank: v.tensor_copy(out=stg2[:], in_=ps[bank][:]), inc=s_ve)
                p.wait("gpsimd", s_ve, s_ve.n)
                p.dma("gpsimd", dbg["o7"], stg2[:], s_dw)
                p.dma("gpsimd", dbg["aoc7"], aoc[:], s_dw)
                p.wait("vector", s_dw)
            p.do("vector", lambda v, co=co, bank=bank: v.tensor_scalar(
                out=stg[:], in0=ps[bank][:], scalar1=bcols[:, off["b2"] + co:off["b2"] + co + 1],
                scalar2=None, op0=ALU.add))
            p.do("vector", lambda v, co=co: v.tensor_tensor(
                out=z2[:, co * 512:(co + 1) * 512], in0=stg[:], in1=aoc[:], op=ALU.add), inc=s_ve)
            z_cp[co] = s_ve.n
            aoc_guard[0] = s_ve.n
        if tb == 0 and DEV["debug_outs"]:
            zdmp = es_f2.enter_context(nc.sbuf_tensor("zdmp", [128, CB * 512], F32))
            for cb in range(CB):
                p.do("vector", lambda v, cb=cb: v.tensor_copy(
                    out=zdmp[:, cb * 512:(cb + 1) * 512], in_=z2[:, cb * 512:(cb + 1) * 512]),
                    inc=(s_ve if cb == CB - 1 else None))
            p.wait("gpsimd", s_ve, s_ve.n)
            p.dma("gpsimd", dbg["z2"].rearrange("cb p t -> p cb t"),
                  zdmp[:].rearrange("p (cb t) -> p cb t", cb=CB), s_dw)
            p.dma("gpsimd", dbg["h0"].rearrange("fb p t -> p fb t"),
                  htb[0][:].rearrange("p (fb t) -> p fb t", fb=NFB), s_dw)
            p.wait("tensor", s_dw)
            p.wait("vector", s_dw)
        feat_ln_affine(z2, 512, "g2", "be2", L1_EPS, dump_rows=(tb == 0))
        zn = s_ve.n
        ln_guard[0] = zn
        p.wait("gpsimd", s_ve, zn)
        for cb in range(CB):
            p.dma("gpsimd", sc_s1[cb, :, tb * 512:(tb + 1) * 512], z2[:, cb * 512:(cb + 1) * 512], s_dw)
        p.wait("tensor", s_dw)
        p.wait("vector", s_dw)
    p.barrier(s_jn)
    es_f2.close()
    if DEV["debug_outs"]:
        p.dma("gpsimd", dbg["s1"].rearrange("(cb p) t -> p cb t", p=128), sc_s1.rearrange("cb p t -> p cb t"), s_dw)
        p.wait("gpsimd", s_dw)

    es_f3 = ExitStack()
    sbf3 = lambda name, shape, dt: es_f3.enter_context(nc.sbuf_tensor(name, shape, dt))
    wq1b = [sbf3(f"wq1b{i}", [128, CB * 128], BF16) for i in range(2)]
    hqstg = [sbf3(f"hqstg{i}", [128, 512], BF16) for i in range(2)]
    hq_dw = {}
    for fb in range(NFB):
        wb = fb % 2
        if fb >= 2:
            p.wait("gpsimd", s_mm, hq_dw.get(("mm", fb - 2), 0))
        p.dma("gpsimd", wq1b[wb][:].rearrange("p (cb f) -> p cb f", cb=CB),
              sc_wq1.rearrange("cb p f -> p cb f")[:, :, fb * 128:(fb + 1) * 128], s_wl)
        p.wait("tensor", s_wl)
        for tb in range(4):
            u = fb * 4 + tb
            bank = u % 2
            if u >= 2:
                p.wait("tensor", s_sc, hq_dw.get(("ev", u - 2), 0))
            for cb in range(CB):
                p.do("tensor", lambda t, cb=cb, tb=tb, wb=wb, bank=bank: t.matmul(
                    out=ps[bank][:], lhsT=wq1b[wb][:, cb * 128:(cb + 1) * 128],
                    rhs=x1nT[:, cb * TOK + tb * 512: cb * TOK + tb * 512 + 512],
                    start=(cb == 0), stop=(cb == CB - 1)),
                    inc=(s_mm if cb == CB - 1 else None))
            if tb == 3:
                hq_dw[("mm", fb)] = s_mm.n
            mm_h = s_mm.n
            p.wait("scalar", s_mm, mm_h)
            if u >= 2:
                p.wait("scalar", s_dw, hq_dw.get(("dw", u - 2), 0))
            p.do("scalar", lambda s, fb=fb, bank=bank, u=u: s.activation(
                out=hqstg[u % 2][:], in_=ps[bank][:], func=AF.Relu,
                bias=bcols[:, off["fb1"] + fb:off["fb1"] + fb + 1]), inc=s_sc)
            hq_dw[("ev", u)] = s_sc.n
            p.wait("gpsimd", s_sc, s_sc.n)
            p.dma("gpsimd", sc_hq[fb, :, tb * 512:(tb + 1) * 512], hqstg[u % 2][:], s_dw)
            hq_dw[("dw", u)] = s_dw.n
    if DEV["debug_outs"]:
        p.dma("gpsimd", dbg["schq"][0], sc_hq[0], s_dw)
        p.dma("gpsimd", dbg["schq"][1], sc_hq[8], s_dw)
        p.wait("gpsimd", s_dw)
    p.barrier(s_jn)
    es_f3.close()

    es_f4 = ExitStack()
    sbf4 = lambda name, shape, dt: es_f4.enter_context(nc.sbuf_tensor(name, shape, dt))
    wq2r = sbf4("wq2r", [128, NFB * D], BF16)
    hqtb = sbf4("hqtb", [128, NFB * 512], BF16)
    s1tb = sbf4("s1tb", [128, CB * 512], F32)
    prod = sbf4("prod", [128, CB * 512], F32)
    x1t_ = sbf4("x1t_", [128, D], F32)
    x2t_ = sbf4("x2t_", [128, D], F32)
    for half in range(4):
        p.dma("gpsimd", wq2r[:].rearrange("p (fb o) -> p fb o", fb=NFB)[:, half * 8:(half + 1) * 8, :],
              sc_wq2.rearrange("fb p o -> p fb o")[:, half * 8:(half + 1) * 8, :], s_wl)
    wq2_ld = s_wl.n
    p.wait("tensor", s_wl, wq2_ld)
    prod_guard = [0]
    for tb in range(4):
        for half in range(4):
            p.wait("gpsimd", s_mm if tb > 0 else s_wl)
            p.dma("gpsimd", hqtb[:].rearrange("p (fb t) -> p fb t", fb=NFB)[:, half * 8:(half + 1) * 8, :],
                  sc_hq[half * 8:(half + 1) * 8, :, tb * 512:(tb + 1) * 512].rearrange("fb p t -> p fb t"), s_wl)
        p.wait("sync", s_ve, prod_guard[0])
        p.dma("sync", s1tb[:].rearrange("p (cb t) -> p cb t", cb=CB), sc_s1[:, :, tb * 512:(tb + 1) * 512].rearrange("cb p t -> p cb t"), s_ld)
        hq_ld = s_wl.n
        s1_ld = s_ld.n
        p.wait("tensor", s_wl, hq_ld)
        g_cp = {}
        for co in range(CB):
            bank = co % 2
            if co < 2:
                p.wait("tensor", s_ve, prod_guard[0])
            if co >= 2:
                p.wait("tensor", s_ve, g_cp[co - 2])
            for fb in range(NFB):
                p.do("tensor", lambda t, co=co, fb=fb, bank=bank: t.matmul(
                    out=ps[bank][:], lhsT=wq2r[:, fb * D + co * 128: fb * D + (co + 1) * 128],
                    rhs=hqtb[:, fb * 512:(fb + 1) * 512],
                    start=(fb == 0), stop=(fb == NFB - 1)),
                    inc=(s_mm if fb == NFB - 1 else None))
            mm_g = s_mm.n
            p.wait("scalar", s_mm, mm_g)
            p.wait("scalar", s_ve, g_cp.get(co - 1, 0))
            p.do("scalar", lambda s, co=co, bank=bank: s.activation(
                out=stg2[:], in_=ps[bank][:], func=AF.Silu,
                bias=bcols[:, off["fb2"] + co:off["fb2"] + co + 1]), inc=s_sc)
            p.wait("vector", s_sc)
            p.wait("vector", s_ld, s1_ld)
            p.do("vector", lambda v, co=co: v.tensor_tensor(
                out=prod[:, co * 512:(co + 1) * 512], in0=s1tb[:, co * 512:(co + 1) * 512],
                in1=stg2[:], op=ALU.mult), inc=s_ve)
            g_cp[co] = s_ve.n
        prod_n = s_ve.n
        prod_guard[0] = prod_n
        if tb == 0 and DEV["debug_outs"]:
            p.wait("gpsimd", s_ve, prod_n)
            p.dma("gpsimd", dbg["prod0"].rearrange("cb p t -> p cb t"),
                  prod[:].rearrange("p (cb t) -> p cb t", cb=CB), s_dw)
            p.dma("gpsimd", dbg["hq0"].rearrange("fb p t -> p fb t"),
                  hqtb[:].rearrange("p (fb t) -> p fb t", fb=NFB), s_dw)
            p.wait("vector", s_dw)
            p.wait("tensor", s_dw)
        p.wait("tensor", s_ve, prod_n)
        for ti in range(4):
            for cb in range(CB):
                p.do("tensor", lambda t, ti=ti, cb=cb: t.transpose(
                    out=pt32[:, cb * 128:(cb + 1) * 128] if False else pt32[:, cb * 128:(cb + 1) * 128],
                    in_=prod[:, cb * 512 + ti * 128: cb * 512 + (ti + 1) * 128],
                    identity=id32[:]), inc=(s_tr if cb == CB - 1 else None))
            tr_n = s_tr.n
            ti_glob = tb * 4 + ti
            p.wait("sync", s_ve, prod_guard[1] if len(prod_guard) > 1 else 0)
            p.dma("sync", x1t_[:], x1[ti_glob * 128:(ti_glob + 1) * 128, :], s_ld)
            x1_ld = s_ld.n
            p.wait("vector", s_tr, tr_n)
            p.wait("vector", s_ld, x1_ld)
            p.do("vector", lambda v: v.tensor_copy(out=x2t_[:, 0:512], in_=pt32[:, 0:512]))
            p.do("vector", lambda v: v.tensor_copy(out=x2t_[:, 512:1024], in_=pt32[:, 512:1024]))
            p.do("vector", lambda v: v.tensor_tensor(out=x2t_[:], in0=x2t_[:], in1=x1t_[:], op=ALU.add), inc=s_ve)
            x2_n = s_ve.n
            p.wait("gpsimd", s_ve, x2_n)
            p.dma("gpsimd", o_x2[ti_glob * 128:(ti_glob + 1) * 128, :], x2t_[:], s_st)
            p.wait("vector", s_st, s_st.n)
            p.wait("tensor", s_ve, x2_n)
            if len(prod_guard) > 1:
                prod_guard[1] = x2_n
            else:
                prod_guard.append(x2_n)
    if DEV["debug_outs"]:
        p.dma("gpsimd", dbg["g1s"].rearrange("(cb p) t -> p cb t", p=128), dbg["g1s"].rearrange("(cb p) t -> p cb t", p=128), s_dw) if False else None
    for e in ENGS:
        p.wait(e, s_st)
    p.barrier(s_jn)
    block = es.enter_context(nc.Block())
    p.emit(block)
    es_f4.close()
    es.close()
    return nc


def _rope_tables():
    pos = np.arange(S, dtype=np.float64)[:, None]
    div = np.exp(np.arange(0, HD, 2, dtype=np.float64) * (-math.log(10000.0) / HD))
    ang = pos * div
    sin = np.tile(np.sin(ang), (1, NH)).astype(np.float32)
    cos = np.tile(np.cos(ang), (1, NH)).astype(np.float32)
    return sin, cos


def host_prep_a(inputs):
    T = lambda a: np.ascontiguousarray(np.asarray(a).T)
    shared = {
        "w_qkvT": T(inputs["L0_attn_w_qkv"]),
        "b_qkv": np.asarray(inputs["L0_attn_b_qkv"]).reshape(1, -1),
        "w_lepe": np.ascontiguousarray(np.asarray(inputs["L0_attn_w_lepe"])[:, 0, :].T),
        "b_lepe": np.asarray(inputs["L0_attn_b_lepe"]).reshape(1, -1),
        "router_wT": T(inputs["L0_ffn_router_w"]),
        "gate_wT": np.ascontiguousarray(np.asarray(inputs["L0_ffn_gate_w"]).transpose(0, 2, 1)),
        "up_wT": np.ascontiguousarray(np.asarray(inputs["L0_ffn_up_w"]).transpose(0, 2, 1)),
        "down_wT": np.ascontiguousarray(np.asarray(inputs["L0_ffn_down_w"]).transpose(0, 2, 1)),
        "wq1": np.ascontiguousarray(np.asarray(inputs["L1_attn_Wq"])),
        "wk1": np.ascontiguousarray(np.asarray(inputs["L1_attn_Wk"])),
        "wv1": np.ascontiguousarray(np.asarray(inputs["L1_attn_Wv"])),
        "bqkv1": np.concatenate([np.asarray(inputs["L1_attn_bq"]),
                                 np.asarray(inputs["L1_attn_bk"]),
                                 np.asarray(inputs["L1_attn_bv"])]).reshape(1, -1),
        "ident16": np.eye(128, dtype=ml_dtypes.bfloat16),
        "ident32": np.eye(128, dtype=np.float32),
        "ones_r": np.ones((1, 128), np.float32),
    }
    rsin, rcos = _rope_tables()
    x = np.asarray(inputs["L0_attn_x"])
    per_core = []
    for c in range(NCORE):
        b, r0 = c // 4, (c % 4) * TOK
        m = dict(shared)
        m["xb"] = np.ascontiguousarray(np.roll(x[b], -r0, axis=0))
        m["rsin"] = np.ascontiguousarray(np.roll(rsin, -r0, axis=0))
        m["rcos"] = np.ascontiguousarray(np.roll(rcos, -r0, axis=0))
        lm = np.ones((128, 2), np.float32)
        if r0 == 0:
            lm[0, 0] = 0.0
        if r0 == S - TOK:
            lm[127, 1] = 0.0
        m["lepe_m"] = lm
        per_core.append(m)
    return per_core


def host_prep_b(inputs, res_a):
    T = lambda a: np.ascontiguousarray(np.asarray(a).T)
    col = lambda a, nb: np.ascontiguousarray(np.asarray(a, dtype=np.float32).reshape(nb, 128).T)
    shared = {
        "wo": np.ascontiguousarray(np.asarray(inputs["L1_attn_Wo"])),
        "w1": np.ascontiguousarray(np.asarray(inputs["L1_attn_W1"])),
        "w2": np.ascontiguousarray(np.asarray(inputs["L1_attn_W2"])),
        "fw1T": T(inputs["L1_ffn_W1"]),
        "fw2T": T(inputs["L1_ffn_W2"]),
        "bo_c": col(inputs["L1_attn_bo"], CB),
        "b1_c": col(inputs["L1_attn_b1"], FF // 128),
        "b2_c": col(inputs["L1_attn_b2"], CB),
        "fb1_c": col(inputs["L1_ffn_b1"], FF // 128),
        "fb2_c": col(inputs["L1_ffn_b2"], CB),
        "g1_c": col(inputs["L1_attn_g1"], CB),
        "be1_c": col(inputs["L1_attn_be1"], CB),
        "g2_c": col(inputs["L1_attn_g2"], CB),
        "be2_c": col(inputs["L1_attn_be2"], CB),
        "ident32": np.eye(128, dtype=np.float32),
        "ones_r": np.ones((1, 128), np.float32),
        "ones_cl": np.ones((128, 1), np.float32),
    }
    per_core = []
    for c in range(NCORE):
        b = c // 4
        m = dict(shared)
        m["x1"] = res_a[c]["o_x1"]
        m["x1nT_in"] = res_a[c]["o_x1nT"]
        m["qnT_in"] = res_a[c]["o_qnT"]
        m["kvp4"] = np.ascontiguousarray(np.stack([res_a[cc]["o_kvp"] for cc in range(4 * b, 4 * b + 4)]))
        per_core.append(m)
    return per_core


def kernel(**inputs):
    per_core_a = host_prep_a(inputs)
    nc_a = build_launch_a()
    res_a = run_bass_kernel_spmd(nc_a, per_core_a, core_ids=list(range(NCORE)))
    per_core_b = host_prep_b(inputs, res_a.results)
    nc_b = build_launch_b()
    res_b = run_bass_kernel_spmd(nc_b, per_core_b, core_ids=list(range(NCORE)))
    out = np.empty((B, S, D), np.float32)
    for c in range(NCORE):
        b, r0 = c // 4, (c % 4) * TOK
        out[b, r0:r0 + TOK] = res_b.results[c]["o_x2"]
    return out
